# version 05
# baseline (speedup 1.0000x reference)
import numpy as np
from contextlib import ExitStack

import concourse.bass as bass
import concourse.mybir as mybir
from concourse.bass_utils import run_bass_kernel_spmd

F32 = mybir.dt.float32
F16 = mybir.dt.float16
BF16 = mybir.dt.bfloat16
AF = mybir.ActivationFunctionType
ALU = mybir.AluOpType
AX = mybir.AxisListType

D = 1024
E = 8
TPC = 2048
NT = 4
TT = 512
NC = 8
NHC = 32

ENGINES = ("sync", "gpsimd", "tensor", "scalar", "vector")
ESEM = {"sync": "sSYE", "gpsimd": "sGE", "tensor": "sTE", "scalar": "sAC",
        "vector": "sVE"}
DMA_SEMS = frozenset(
    ["lConst", "lP0", "lP1", "lP2", "lSA", "lSB0", "lSB1", "lS0", "lT",
     "lFix", "stN", "stW", "stXa", "stY", "stCC1", "stCC2", "stDbg",
     "stPW"])

DEBUG = False


class KB:

    def __init__(self, sems):
        self.sems = sems
        self.streams = {e: [] for e in ENGINES}
        self.cnt = {s: 0 for s in sems}
        self.wm = {e: {s: 0 for s in sems} for e in ENGINES}
        self.buf = {}

    def _wait(self, eng, tag):
        sem, val = tag
        if sem in DMA_SEMS:
            val = self.cnt[sem]
        if self.wm[eng][sem] >= val:
            return
        self.wm[eng][sem] = val
        h = self.sems[sem]
        self.streams[eng].append(lambda e, h=h, v=val: e.wait_ge(h, v))

    def op(self, eng, fns, reads=(), writes=(), inc=None, sem=None):
        if callable(fns):
            fns = [fns]
        for b in reads:
            st = self.buf.get(b)
            if st and st["prod"]:
                self._wait(eng, st["prod"])
        for b in writes:
            st = self.buf.get(b)
            if st:
                if st["prod"]:
                    self._wait(eng, st["prod"])
                for t in st["readers"]:
                    self._wait(eng, t)
        semname = sem or ESEM[eng]
        if inc is None:
            inc = 16 if semname in DMA_SEMS else 1
        self.cnt[semname] += inc
        tag = (semname, self.cnt[semname])
        h = self.sems[semname]
        for fn in fns[:-1]:
            self.streams[eng].append(fn)
        self.streams[eng].append(
            lambda e, fn=fns[-1], h=h, n=inc: fn(e).then_inc(h, n))
        for b in reads:
            self.buf.setdefault(b, {"prod": None, "readers": []})[
                "readers"].append(tag)
        for b in writes:
            self.buf[b] = {"prod": tag, "readers": []}
        return tag


def build_program(debug=False):
    nc = bass.Bass("TRN2", target_bir_lowering=False)

    xd = nc.dram_tensor("x", [128, NC, TPC], F32, kind="ExternalInput").ap()
    wkd = nc.dram_tensor("wk", [NC, 128, NC, 128], F32, kind="ExternalInput").ap()
    wvd = nc.dram_tensor("wv", [NC, 128, NC, 128], F32, kind="ExternalInput").ap()
    wrd = nc.dram_tensor("wr", [NC, 128, NC, 128], F32, kind="ExternalInput").ap()
    wod = nc.dram_tensor("wo", [NC, 128, NC, 128], F32, kind="ExternalInput").ap()
    wfcd = nc.dram_tensor("wfc", [NHC, 128, NC, 128], F32, kind="ExternalInput").ap()
    wmlpd = nc.dram_tensor("wmlp", [NC, 128, NHC, 128], F32, kind="ExternalInput").ap()
    w1d = nc.dram_tensor("w1", [E, NC, 128, NC, 128], F32, kind="ExternalInput").ap()
    w2d = nc.dram_tensor("w2", [E, NC, 128, NC, 128], F32, kind="ExternalInput").ap()
    rwd = nc.dram_tensor("rw", [128, NC, E], F32, kind="ExternalInput").ap()
    b2rd = nc.dram_tensor("b2r", [E, D], F32, kind="ExternalInput").ap()
    b1vd = nc.dram_tensor("b1v", [128, 64], F32, kind="ExternalInput").ap()
    vecsd = nc.dram_tensor("vecs", [128, 32], F32, kind="ExternalInput").ap()
    rbd = nc.dram_tensor("rb", [E, 1], F32, kind="ExternalInput").ap()
    identd = nc.dram_tensor("ident", [128, 128], F32, kind="ExternalInput").ap()
    eseld = nc.dram_tensor("esel", [E, E * 128], F32, kind="ExternalInput").ap()
    sseld = nc.dram_tensor("ssel", [128, E], F32, kind="ExternalInput").ap()
    fixgd = nc.dram_tensor("fixg", [2, 1], F32, kind="ExternalInput").ap()

    yd = nc.dram_tensor("y", [128, NC, TPC], F32, kind="ExternalOutput").ap()

    pwk = nc.dram_tensor("pwk", [NC, 128, 2048], F16).ap()
    pwv = nc.dram_tensor("pwv", [NC, 128, 2048], F16).ap()
    pwr = nc.dram_tensor("pwr", [NC, 128, 2048], F16).ap()
    pwo = nc.dram_tensor("pwo", [NC, 128, 2048], F16).ap()
    pwfc = nc.dram_tensor("pwfc", [NHC, 128, 2048], F16).ap()
    pwmlp = nc.dram_tensor("pwmlp", [NC, 2, 2, 128, 2048], F16).ap()
    pw1 = nc.dram_tensor("pw1", [E, NC, 128, 2048], F16).ap()
    pw2 = nc.dram_tensor("pw2", [E, NC, 128, 2048], BF16).ap()
    dnormh = nc.dram_tensor("dnormh", [128, NC, TPC], F16).ap()
    dnorml = nc.dram_tensor("dnorml", [128, NC, TPC], F16).ap()
    dwkv = nc.dram_tensor("dwkv", [128, NC, TPC], F32).ap()
    dxa = nc.dram_tensor("dxa", [128, NC, TPC], F32).ap()
    cc1i = nc.dram_tensor("cc1i", [1, D], F32).ap()
    cc1o = nc.dram_tensor("cc1o", [NC, D], F32).ap()
    cc2i = nc.dram_tensor("cc2i", [1, D], F32).ap()
    cc2o = nc.dram_tensor("cc2o", [NC, D], F32).ap()

    dbg = {}
    if debug:
        for nm in ("dbg_normed", "dbg_wkv", "dbg_m", "dbg_x1", "dbg_xa",
                   "dbg_macc", "dbg_k", "dbg_kv"):
            dbg[nm] = nc.dram_tensor(nm, [128, NC, TT], F32,
                                     kind="ExternalOutput").ap()
        dbg["dbg_maskem"] = nc.dram_tensor("dbg_maskem", [E, TPC], F32,
                                           kind="ExternalOutput").ap()
        dbg["dbg_ltm"] = nc.dram_tensor("dbg_ltm", [128, 128], F32,
                                        kind="ExternalOutput").ap()
        dbg["dbg_scan"] = nc.dram_tensor("dbg_scan", [128, TT + 1], F32,
                                         kind="ExternalOutput").ap()
        dbg["dbg_wkv0"] = nc.dram_tensor("dbg_wkv0", [128, TT], F32,
                                         kind="ExternalOutput").ap()

    with ExitStack() as ctx:
        sb = lambda name, shape: ctx.enter_context(
            nc.sbuf_tensor(name, shape, F32))
        P = [sb(f"p{i}", [128, NC * TT]) for i in range(4)]
        sbh = lambda name, shape: ctx.enter_context(
            nc.sbuf_tensor(name, shape, F16))
        SP16 = sbh("sp16", [128, 2 * 16 * TT])
        Ghi = SP16[:, 0:16 * TT]
        Glo = SP16[:, 16 * TT:32 * TT]
        h2hi = SP16[:, 0:NC * TT].bitcast(BF16)
        h2lo = SP16[:, NC * TT:2 * NC * TT].bitcast(BF16)
        sphi = sbh("sphi", [128, NC * TT])
        splo = sbh("splo", [128, NC * TT])
        wsp = [(sbh("wsp0h", [128, 1024]), sbh("wsp0l", [128, 1024])),
               (sbh("wsp1h", [128, 1024]), sbh("wsp1l", [128, 1024]))]
        wspb = [(wsp[0][0].bitcast(BF16), wsp[0][1].bitcast(BF16)),
                (wsp[1][0].bitcast(BF16), wsp[1][1].bitcast(BF16))]
        stripB = [sb("stripB0", [128, 1024]), sb("stripB1", [128, 1024])]
        ktmp = sb("ktmpS", [128, TT])
        rtmp = sb("rtmpS", [128, TT])
        ident = sb("identS", [128, 128])
        esel = sb("eselS", [E, E * 128])
        ones1 = sb("ones1", [1, 128])
        onesK = sb("onesK", [128, 1])
        vecs = sb("vecsS", [128, 32])
        ew = sb("ewS", [128, NC])
        eu = sb("euS", [128, NC])
        ne1 = sb("ne1S", [128, NC])
        sstate = sb("sstateS", [128, NC])
        s0 = sb("s0S", [128, NC])
        s0raw = sb("s0rawS", [128, NC * NC])
        ssel = sb("sselS", [128, E])
        scanbuf = sb("scanbufS", [128, TT + 1])
        rampb = sb("rampS", [128, TT])
        tmpb = sb("tmpS", [128, TT])
        tmp2b = sb("tmp2S", [128, TT])
        rwsb = sb("rwS", [128, NC * E])
        rbsb = sb("rbS", [E, 1])
        b1v = sb("b1vS", [128, 64])
        b2r = sb("b2rS", [E, D])
        logits = sb("logitsS", [E, TT])
        ltm = sb("ltmS", [128, 128])
        uu = sb("uuS", [128, 128])
        eq1 = sb("eq1S", [128, 128])
        eq2 = sb("eq2S", [128, 128])
        a01 = sb("a01S", [128, 256])
        masktm = sb("masktmS", [128, 128])
        maskem = sb("maskemS", [E, TPC])
        maskb = sb("maskbS", [128, TT])
        rowm = sb("rowmS", [128, 16])
        rowm2 = sb("rowm2S", [128, 16])
        roww = sb("rowwS", [128, 48])
        statrow = sb("statrowS", [1, 4 * TT])
        bcA = sb("bcAS", [128, TT])
        bcB = sb("bcBS", [128, TT])
        trow = sb("trowS", [16, NC])
        tcol = sb("tcolS", [16, 1])
        tpad = sb("tpadS", [1, D])
        fixem = sb("fixemS", [2, E])
        fixg = sb("fixgS", [2, 1])
        epsr = sb("epsrS", [1, 2])

        ps = [ctx.enter_context(nc.psum_tensor(f"ps{i}", [128, TT], F32))
              for i in range(6)]
        psRow = ctx.enter_context(nc.psum_tensor("psRow", [128, TT], F32))
        psT = ctx.enter_context(nc.psum_tensor("psT", [128, TT], F32))
        psStat = psRow[0:1, :]

        semnames = (["sTE", "sAC", "sVE", "sGE", "sCC", "sSYE"]
                    + sorted(DMA_SEMS))
        sems = {s: ctx.enter_context(nc.semaphore(s)) for s in semnames}
        kb = KB(sems)

        def plane(p, c):
            return p[:, c * TT:(c + 1) * TT]

        def v3(ap, a=16):
            return ap.rearrange("p (a b) -> p a b", a=a)

        def load(dst_ap, src_ap, writes, reads=(), slow=False, sem="lConst"):
            kb.op("sync", lambda e, d=dst_ap, s=src_ap, sl_=slow: e.dma_start(
                out=d, in_=s, allow_slow_non_contiguous=sl_),
                reads=reads, writes=writes, sem=sem)

        def store(dst_ap, src_ap, reads, writes=(), slow=False, sem="stDbg"):
            kb.op("gpsimd", lambda e, d=dst_ap, s=src_ap, sl_=slow: e.dma_start(
                out=d, in_=s, allow_slow_non_contiguous=sl_),
                reads=reads, writes=writes, sem=sem)

        def strip_load(buf_i, src):
            sbuf = stripB[buf_i]
            load(sbuf[:, :].rearrange("p (c m) -> p c m", c=NC), src,
                 (f"stripB{buf_i}",), sem=f"lSB{buf_i}")

        def mm_chain(bank_i, strip, rhs_of_kc, nk, reads, extra=()):
            bank = ps[bank_i]
            total = nk + len(extra)
            fns = []
            for kc in range(nk):
                fns.append(lambda t, kc=kc, bank=bank, strip=strip: t.matmul(
                    bank[:, :], strip[:, kc * 128:(kc + 1) * 128],
                    rhs_of_kc(kc), start=(kc == 0),
                    stop=(kc == total - 1)))
            for i, (lh, rh) in enumerate(extra):
                fns.append(lambda t, lh=lh, rh=rh, i=i, bank=bank: t.matmul(
                    bank[:, :], lh, rh, start=False,
                    stop=(nk + i == total - 1)))
            return kb.op("tensor", fns, reads=reads, writes=(f"ps{bank_i}",))

        def split16(src_ap, hi_ap, lo_ap, reads, wname):
            kb.op("scalar", lambda e, s=src_ap, h=hi_ap: e.copy(h, s),
                  reads=reads, writes=(wname,))
            kb.op("vector", lambda e, s=src_ap, h=hi_ap, l=lo_ap:
                  e.scalar_tensor_tensor(l, h, -1.0, s, ALU.mult, ALU.add),
                  reads=reads + (wname,), writes=(wname,))

        def split_strip(i, bf=False):
            pair = wspb[i] if bf else wsp[i]
            split16(stripB[i][:, :], pair[0][:, :], pair[1][:, :],
                    (f"stripB{i}",), f"wsp{i}")

        presplit_done = set()

        def strip3(i, key, fp32_src, pre, bf=False):
            pair = wspb[i] if bf else wsp[i]
            if key not in presplit_done:
                presplit_done.add(key)
                strip_load(i, fp32_src)
                split_strip(i, bf=bf)
                store(pre[:, 0:1024], pair[0][:, :], (f"wsp{i}",), (key,),
                      sem="stPW")
                store(pre[:, 1024:2048], pair[1][:, :], (f"wsp{i}",), (key,),
                      sem="stPW")
            else:
                load(pair[0][:, :], pre[:, 0:1024], (f"wsp{i}",),
                     reads=(key,), sem=f"lSB{i}")
                load(pair[1][:, :], pre[:, 1024:2048], (f"wsp{i}",),
                     reads=(key,), sem=f"lSB{i}")

        def mm3(bank_i, whi, wlo, xhi_of, xlo_of, nk, reads, extra=()):
            bank = ps[bank_i]
            total = 3 * nk + len(extra)
            fns = []
            n = 0
            for kc in range(nk):
                for (wv_, xv) in ((whi, xhi_of(kc)), (whi, xlo_of(kc)),
                                  (wlo, xhi_of(kc))):
                    fns.append(lambda t, w=wv_, x=xv, kc=kc, n=n, bank=bank:
                               t.matmul(bank[:, :],
                                        w[:, kc * 128:(kc + 1) * 128], x,
                                        start=(n == 0),
                                        stop=(n == total - 1)))
                    n += 1
            for (lh, rh) in extra:
                fns.append(lambda t, lh=lh, rh=rh, n=n, bank=bank: t.matmul(
                    bank[:, :], lh, rh, start=False, stop=(n == total - 1)))
                n += 1
            return kb.op("tensor", fns, reads=reads, writes=(f"ps{bank_i}",))

        load(ident[:, :], identd, ("ident",))
        load(esel[:, :], eseld, ("esel",))
        load(vecs[:, :], vecsd, ("vecs",))
        load(rwsb[:, :].rearrange("p (c e) -> p c e", c=NC), rwd, ("rw",))
        load(rbsb[:, :], rbd, ("rb",))
        load(b1v[:, :], b1vd, ("b1v",))
        load(b2r[:, :], b2rd, ("b2r",))
        load(ssel[:, :], sseld, ("ssel",))
        load(fixg[:, :], fixgd, ("fixg",))
        kb.op("vector", lambda e: e.memset(ones1[:, :], 1.0), writes=("ones1",))
        kb.op("vector", lambda e: e.memset(onesK[:, :], 1.0), writes=("onesK",))
        kb.op("vector", lambda e: e.memset(sstate[:, :], 0.0),
              writes=tuple(f"sst{c}" for c in range(NC)))
        kb.op("vector", lambda e: e.memset(tpad[:, :], 0.0), writes=("tpad",))
        kb.op("vector", lambda e: e.memset(epsr[:, 0:1], 1e-6), writes=("epsr",))
        kb.op("vector", lambda e: e.memset(epsr[:, 1:2], 1e-5), writes=("epsr",))
        kb.op("scalar", lambda e: e.activation(ne1[:, :], vecs[:, 0:8], AF.Exp),
              reads=("vecs",), writes=("ne1",))
        kb.op("scalar", lambda e: e.activation(ew[:, :], ne1[:, :], AF.Exp,
                                               scale=-1.0),
              reads=("ne1",), writes=("ew",))
        kb.op("scalar", lambda e: e.activation(eu[:, :], vecs[:, 8:16], AF.Exp),
              reads=("vecs",), writes=("eu",))
        kb.op("vector", lambda e: e.tensor_scalar_mul(ne1[:, :], ne1[:, :],
                                                      -1.0),
              reads=("ne1",), writes=("ne1",))

        mxs = vecs[:, 16:24]
        mls = vecs[:, 24:32]

        def sq_chain(src_p, sname):
            bufs = (tmpb, tmp2b)
            names = ("tmpb", "tmp2b")
            for c in range(NC):
                kb.op("scalar", lambda e, c=c, b=bufs[c % 2]: e.activation(
                    b[:, :], plane(src_p, c), AF.Square),
                    reads=(sname,), writes=(names[c % 2],))
                kb.op("tensor", lambda t, c=c, b=bufs[c % 2]: t.matmul(
                    psStat, onesK[:, :], b[:, :],
                    start=(c == 0), stop=(c == NC - 1)),
                    reads=(names[c % 2], "onesK"), writes=("psRow",))

        def rms_stats(src_p, sname):
            sq_chain(src_p, sname)
            rms_finish()

        def rms_finish():
            kb.op("vector", lambda e: e.tensor_scalar(
                statrow[:, 0:TT], psStat, 1.0 / D, 1e-6, ALU.mult, ALU.add),
                reads=("psRow",), writes=("statrow",))
            kb.op("scalar", lambda e: e.activation(
                statrow[:, 2 * TT:3 * TT], statrow[:, 0:TT], AF.Ln),
                reads=("statrow",), writes=("statrow",))
            kb.op("scalar", lambda e: e.activation(
                statrow[:, TT:2 * TT], statrow[:, 2 * TT:3 * TT], AF.Exp,
                scale=-0.5),
                reads=("statrow",), writes=("statrow",))
            for _nr in range(1):
                kb.op("vector", lambda e: e.tensor_mul(
                    statrow[:, 2 * TT:3 * TT], statrow[:, 0:TT],
                    statrow[:, TT:2 * TT]),
                    reads=("statrow",), writes=("statrow",))
                kb.op("vector", lambda e: e.tensor_mul(
                    statrow[:, 2 * TT:3 * TT], statrow[:, 2 * TT:3 * TT],
                    statrow[:, TT:2 * TT]),
                    reads=("statrow",), writes=("statrow",))
                kb.op("vector", lambda e: e.tensor_scalar(
                    statrow[:, 2 * TT:3 * TT], statrow[:, 2 * TT:3 * TT],
                    -0.5, 1.5, ALU.mult, ALU.add),
                    reads=("statrow",), writes=("statrow",))
                kb.op("vector", lambda e: e.tensor_mul(
                    statrow[:, TT:2 * TT], statrow[:, TT:2 * TT],
                    statrow[:, 2 * TT:3 * TT]),
                    reads=("statrow",), writes=("statrow",))
            kb.op("tensor", lambda t: t.matmul(psRow[:, :], ones1[:, :],
                                               statrow[:, TT:2 * TT],
                                               start=True, stop=True),
                  reads=("statrow", "ones1"), writes=("psRow",))
            kb.op("scalar", lambda e: e.copy(bcA[:, :], psRow[:, :]),
                  reads=("psRow",), writes=("bcA",))

        def layernorm(src_p, sname, dst_p, dname):
            fns = [lambda t, c=c: t.matmul(psStat, onesK[:, :],
                                           plane(src_p, c),
                                           start=(c == 0), stop=(c == NC - 1))
                   for c in range(NC)]
            kb.op("tensor", fns, reads=(sname, "onesK"), writes=("psRow",))
            kb.op("vector", lambda e: e.tensor_scalar_mul(
                statrow[:, 0:TT], psStat, 1.0 / D),
                reads=("psRow",), writes=("statrow",))
            sq_chain(src_p, sname)
            ln_rest(src_p, sname, dst_p, dname, psStat, "psRow")

        def ln_finish(src_p, sname, dst_p, dname):
            kb.op("vector", lambda e: e.tensor_scalar_mul(
                statrow[:, 0:TT], psStat, 1.0 / D),
                reads=("psRow",), writes=("statrow",))
            ln_rest(src_p, sname, dst_p, dname, psT[0:1, 0:TT], "psT")

        def ln_rest(src_p, sname, dst_p, dname, s2row, s2name):
            kb.op("vector", lambda e: e.tensor_mul(
                statrow[:, TT:2 * TT], statrow[:, 0:TT], statrow[:, 0:TT]),
                reads=("statrow",), writes=("statrow",))
            kb.op("vector", lambda e, s2row=s2row: e.scalar_tensor_tensor(
                statrow[:, TT:2 * TT], s2row, 1.0 / D,
                statrow[:, TT:2 * TT], ALU.mult, ALU.subtract),
                reads=(s2name, "statrow"), writes=("statrow",))
            kb.op("vector", lambda e: e.tensor_scalar(
                statrow[:, TT:2 * TT], statrow[:, TT:2 * TT],
                1.0, 1e-5, ALU.mult, ALU.add),
                reads=("statrow",), writes=("statrow",))
            kb.op("scalar", lambda e: e.activation(
                statrow[:, 2 * TT:3 * TT], statrow[:, TT:2 * TT], AF.Ln),
                reads=("statrow",), writes=("statrow",))
            kb.op("scalar", lambda e: e.activation(
                statrow[:, 3 * TT:4 * TT], statrow[:, 2 * TT:3 * TT], AF.Exp,
                scale=-0.5),
                reads=("statrow",), writes=("statrow",))
            for _nr in range(1):
                kb.op("vector", lambda e: e.tensor_mul(
                    statrow[:, 2 * TT:3 * TT], statrow[:, TT:2 * TT],
                    statrow[:, 3 * TT:4 * TT]),
                    reads=("statrow",), writes=("statrow",))
                kb.op("vector", lambda e: e.tensor_mul(
                    statrow[:, 2 * TT:3 * TT], statrow[:, 2 * TT:3 * TT],
                    statrow[:, 3 * TT:4 * TT]),
                    reads=("statrow",), writes=("statrow",))
                kb.op("vector", lambda e: e.tensor_scalar(
                    statrow[:, 2 * TT:3 * TT], statrow[:, 2 * TT:3 * TT],
                    -0.5, 1.5, ALU.mult, ALU.add),
                    reads=("statrow",), writes=("statrow",))
                kb.op("vector", lambda e: e.tensor_mul(
                    statrow[:, 3 * TT:4 * TT], statrow[:, 3 * TT:4 * TT],
                    statrow[:, 2 * TT:3 * TT]),
                    reads=("statrow",), writes=("statrow",))
            kb.op("vector", lambda e: e.tensor_mul(
                statrow[:, 2 * TT:3 * TT], statrow[:, 0:TT],
                statrow[:, 3 * TT:4 * TT]),
                reads=("statrow",), writes=("statrow",))
            kb.op("tensor", lambda t: t.matmul(psRow[:, :], ones1[:, :],
                                               statrow[:, 3 * TT:4 * TT],
                                               start=True, stop=True),
                  reads=("statrow", "ones1"), writes=("psRow",))
            kb.op("scalar", lambda e: e.copy(bcA[:, :], psRow[:, :]),
                  reads=("psRow",), writes=("bcA",))
            kb.op("tensor", lambda t: t.matmul(psRow[:, :], ones1[:, :],
                                               statrow[:, 2 * TT:3 * TT],
                                               start=True, stop=True),
                  reads=("statrow", "ones1"), writes=("psRow",))
            kb.op("scalar", lambda e: e.copy(bcB[:, :], psRow[:, :]),
                  reads=("psRow",), writes=("bcB",))
            for c in range(NC):
                kb.op("vector", lambda e, c=c: e.tensor_mul(
                    tmpb[:, :], plane(src_p, c), bcA[:, :]),
                    reads=(sname, "bcA"), writes=("tmpb",))
                kb.op("vector", lambda e, c=c: e.tensor_sub(
                    plane(dst_p, c), tmpb[:, :], bcB[:, :]),
                    reads=("tmpb", "bcB"), writes=(dname,))

        for t in range(NT):
            sl = slice(t * TT, (t + 1) * TT)
            a1x, a1kv, a1wkv, a1k = P[0], P[1], P[2], P[3]
            load(a1x[:, :].rearrange("p (c j) -> p c j", c=NC),
                 xd[:, :, sl], ("P0",), sem="lP0")
            rms_stats(a1x, "P0")
            for c in range(NC):
                kb.op("vector", lambda e, c=c, a1x=a1x: e.tensor_mul(
                    tmpb[:, :], plane(a1x, c), bcA[:, :]),
                    reads=("P0", "bcA"), writes=("tmpb",))
                split16(tmpb[:, :], plane(sphi, c), plane(splo, c),
                        ("tmpb",), "sp")
            store(dnormh[:, :, sl],
                  sphi[:, :].rearrange("p (c j) -> p c j", c=NC),
                  ("sp",), (f"dnorm{t}",), sem="stN")
            store(dnorml[:, :, sl],
                  splo[:, :].rearrange("p (c j) -> p c j", c=NC),
                  ("sp",), (f"dnorm{t}",), sem="stN")
            for mc in range(NC):
                strip3(0, f"pwk{mc}", wkd[mc], pwk[mc])
                mm3((2 * mc) % 4, wsp[0][0], wsp[0][1],
                    lambda kc: plane(sphi, kc), lambda kc: plane(splo, kc),
                    NC, ("wsp0", "sp"))
                kb.op("scalar", lambda e, b=(2 * mc) % 4: e.copy(
                    ktmp[:, :], ps[b][:, :]),
                    reads=(f"ps{(2 * mc) % 4}",), writes=("ktmp",))
                strip3(1, f"pwv{mc}", wvd[mc], pwv[mc])
                mm3((2 * mc + 1) % 4, wsp[1][0], wsp[1][1],
                    lambda kc: plane(sphi, kc), lambda kc: plane(splo, kc),
                    NC, ("wsp1", "sp"))
                kb.op("vector", lambda e, mc=mc, b=(2 * mc + 1) % 4,
                      a1kv=a1kv: e.tensor_mul(
                    plane(a1kv, mc), ps[b][:, :], ktmp[:, :]),
                    reads=(f"ps{(2 * mc + 1) % 4}", "ktmp"),
                    writes=(f"P1_{mc}",))
                c = mc
                stn = f"sst{c}"
                kvn = f"P1_{c}"
                kb.op("vector", lambda e, c=c: e.tensor_copy(
                    scanbuf[:, 0:1], sstate[:, c:c + 1]),
                    reads=(stn,), writes=("scanbuf",))
                kb.op("vector", lambda e, c=c, a1wkv=a1wkv, a1kv=a1kv:
                      e.scalar_tensor_tensor(
                    plane(a1wkv, c)[:, 0:1], plane(a1kv, c)[:, 0:1],
                    eu[:, c:c + 1], scanbuf[:, 0:1], ALU.mult, ALU.add),
                    reads=(kvn, "eu", "scanbuf"), writes=(f"P2_{c}",))
                kb.op("vector", lambda e, c=c, a1kv=a1kv:
                      e.scalar_tensor_tensor(
                    plane(a1kv, c)[:, 0:1], sstate[:, c:c + 1],
                    ew[:, c:c + 1], plane(a1kv, c)[:, 0:1],
                    ALU.mult, ALU.add),
                    reads=(stn, "ew", kvn), writes=(kvn,))
                kb.op("vector", [
                    lambda e: e.drain(),
                    lambda e, c=c, a1kv=a1kv: e.tensor_tensor_scan(
                        out=scanbuf[:, 1:TT + 1],
                        data0=ew[:, c:c + 1].broadcast_to((128, TT)),
                        data1=plane(a1kv, c),
                        initial=0.0,
                        op0=ALU.mult, op1=ALU.add),
                    lambda e: e.drain(),
                    lambda e, c=c: e.tensor_copy(
                        sstate[:, c:c + 1], scanbuf[:, TT:TT + 1])],
                    reads=("ew", kvn), writes=("scanbuf", stn))
                kb.op("vector", lambda e, c=c, a1wkv=a1wkv, a1kv=a1kv:
                      e.scalar_tensor_tensor(
                    plane(a1wkv, c)[:, 1:TT], plane(a1kv, c)[:, 1:TT],
                    eu[:, c:c + 1], scanbuf[:, 1:TT], ALU.mult, ALU.add),
                    reads=(kvn, "eu", "scanbuf"), writes=(f"P2_{c}",))
            store(dwkv[:, :, sl], a1wkv[:, :].rearrange("p (c j) -> p c j", c=NC),
                  tuple(f"P2_{c}" for c in range(NC)), (f"dwkv{t}",),
                  sem="stW")

        store(cc1i[0, :].rearrange("(c p) -> p c", p=128), sstate[:, :],
              tuple(f"sst{c}" for c in range(NC)), ("cc1i",), slow=True,
              sem="stCC1")
        kb.op("gpsimd", lambda e: e.collective_compute(
            "AllGather", ALU.bypass, replica_groups=[list(range(8))],
            ins=[cc1i.opt()], outs=[cc1o.opt()]),
            reads=("cc1i",), writes=("cc1o",), sem="sCC", inc=1)
        for r_ in range(NC):
            load(s0raw[:, :].rearrange("p (c r) -> p c r", c=NC)[:, :, r_],
                 cc1o[r_, :].rearrange("(c p) -> p c", p=128),
                 ("s0raw",), reads=("cc1o",), slow=True, sem="lS0")
        kb.op("vector", lambda e: e.tensor_tensor(
            v3(s0raw[:, :], a=NC), v3(s0raw[:, :], a=NC),
            ssel[:, :].unsqueeze(1).broadcast_to((128, NC, E)),
            ALU.mult), reads=("s0raw", "ssel"), writes=("s0raw",))
        kb.op("vector", lambda e: e.reduce_sum(
            s0[:, :], v3(s0raw[:, :], a=NC), axis=AX.X),
            reads=("s0raw",), writes=("s0",))

        routers = []
        for t in range(NT):
            sl = slice(t * TT, (t + 1) * TT)
            a2x, a2wkv, a2x1, a2r = P[0], P[1], P[2], P[3]
            load(a2x[:, :].rearrange("p (c j) -> p c j", c=NC),
                 xd[:, :, sl], ("P0",), sem="lP0")
            load(sphi[:, :].rearrange("p (c j) -> p c j", c=NC),
                 dnormh[:, :, sl], ("sp",), reads=(f"dnorm{t}",), sem="lP1")
            load(splo[:, :].rearrange("p (c j) -> p c j", c=NC),
                 dnorml[:, :, sl], ("sp",), reads=(f"dnorm{t}",), sem="lP1")
            load(a2wkv[:, :].rearrange("p (c j) -> p c j", c=NC),
                 dwkv[:, :, sl], ("P1",), reads=(f"dwkv{t}",), sem="lP2")
            kb.op("gpsimd", lambda e, t=t: e.iota(
                rampb[:, :], pattern=[[1, TT]], base=t * TT,
                channel_multiplier=0, allow_small_or_imprecise_dtypes=True),
                writes=("rampb",), sem="sGE", inc=1)
            def r_chain(mc, rdst, rdstname):
                strip3(mc % 2, f"pwr{mc}", wrd[mc], pwr[mc])
                mm3(mc % 4, wsp[mc % 2][0], wsp[mc % 2][1],
                    lambda kc: plane(sphi, kc), lambda kc: plane(splo, kc),
                    NC, (f"wsp{mc % 2}", "sp"))
                kb.op("scalar", lambda e, b=mc % 4, rdst=rdst:
                      e.activation(rdst, ps[b][:, :], AF.Sigmoid),
                      reads=(f"ps{mc % 4}",), writes=(rdstname,))

            def corr_chunk(c, rsrc, rsrcname):
                kb.op("vector", lambda e, c=c: e.tensor_scalar(
                    tmpb[:, :], rampb[:, :], ne1[:, c:c + 1], None, ALU.mult),
                    reads=("rampb", "ne1"), writes=("tmpb",))
                kb.op("scalar", lambda e: e.activation(
                    tmpb[:, :], tmpb[:, :], AF.Exp),
                    reads=("tmpb",), writes=("tmpb",))
                kb.op("vector", lambda e, c=c, a2wkv=a2wkv:
                      e.scalar_tensor_tensor(
                    tmp2b[:, :], tmpb[:, :], s0[:, c:c + 1],
                    plane(a2wkv, c), ALU.mult, ALU.add),
                    reads=("tmpb", "s0", "P1"), writes=("tmp2b",))
                kb.op("vector", lambda e, rsrc=rsrc: e.tensor_mul(
                    tmpb[:, :], rsrc, tmp2b[:, :]),
                    reads=(rsrcname, "tmp2b"), writes=("tmpb",))
                split16(tmpb[:, :], SP16[:, c * TT:(c + 1) * TT],
                        SP16[:, NC * TT + c * TT:NC * TT + (c + 1) * TT],
                        ("tmpb",), "SP16")

            if t == 0:
                for mc in range(NC):
                    r_chain(mc, plane(a2r, mc), "P3")
                for c in range(NC):
                    corr_chunk(c, plane(a2r, c), "P3")
            else:
                for mc in range(NC):
                    r_chain(mc, rtmp[:, :], "rtmp")
                    corr_chunk(mc, rtmp[:, :], "rtmp")
            if routers:
                routers.pop(0)()
            for mc in range(NC):
                strip3(mc % 2, f"pwo{mc}", wod[mc], pwo[mc])
                mm3(mc % 4, wsp[mc % 2][0], wsp[mc % 2][1],
                    lambda kc: SP16[:, kc * TT:(kc + 1) * TT],
                    lambda kc: SP16[:, NC * TT + kc * TT:
                                    NC * TT + (kc + 1) * TT],
                    NC, (f"wsp{mc % 2}", "SP16"))
                kb.op("vector", lambda e, mc=mc, b=mc % 4, a2x1=a2x1,
                      a2x=a2x: e.scalar_tensor_tensor(
                    plane(a2x1, mc), ps[b][:, :], mxs[:, mc:mc + 1],
                    plane(a2x, mc), ALU.mult, ALU.add),
                    reads=(f"ps{mc % 4}", "vecs", "P0"), writes=("P2",))
                bufs = (tmpb, tmp2b)
                names = ("tmpb", "tmp2b")
                kb.op("scalar", lambda e, b=bufs[mc % 2], a2x1=a2x1, mc=mc:
                      e.activation(b[:, :], plane(a2x1, mc), AF.Square),
                      reads=("P2",), writes=(names[mc % 2],))
                kb.op("tensor", lambda t_, mc=mc, b=bufs[mc % 2]: t_.matmul(
                    psStat, onesK[:, :], b[:, :],
                    start=(mc == 0), stop=(mc == NC - 1)),
                    reads=(names[mc % 2], "onesK"), writes=("psRow",))
            if debug and t == 0:
                store(dbg["dbg_x1"], a2x1[:, :].rearrange(
                    "p (c j) -> p c j", c=NC), ("P2",))
            rms_finish()
            for c in range(NC):
                kb.op("vector", lambda e, c=c, a2x1=a2x1: e.tensor_mul(
                    tmpb[:, :], plane(a2x1, c), bcA[:, :]),
                    reads=("P2", "bcA"), writes=("tmpb",))
                split16(tmpb[:, :], plane(sphi, c), plane(splo, c),
                        ("tmpb",), "sp")
            for half in range(2):
                for h16 in range(16):
                    hk = half * 16 + h16
                    strip3(hk % 2, f"pwfc{hk}", wfcd[hk], pwfc[hk])
                    mm3(hk % 4, wsp[hk % 2][0], wsp[hk % 2][1],
                        lambda kc: plane(sphi, kc), lambda kc: plane(splo, kc),
                        NC, (f"wsp{hk % 2}", "sp"))
                    kb.op("scalar", lambda e, b=hk % 4: e.copy(
                        tmpb[:, :], ps[b][:, :]),
                        reads=(f"ps{hk % 4}",), writes=("tmpb",))
                    kb.op("vector", lambda e: e.scalar_tensor_tensor(
                        tmpb[:, :], tmpb[:, :], 0.5, tmpb[:, :], ALU.mult,
                        ALU.max), reads=("tmpb",), writes=("tmpb",))
                    kb.op("vector", lambda e: e.tensor_mul(
                        tmp2b[:, :], tmpb[:, :], tmpb[:, :]),
                        reads=("tmpb",), writes=("tmp2b",))
                    kb.op("vector", lambda e, h16=h16: e.tensor_copy(
                        Ghi[:, h16 * TT:(h16 + 1) * TT], tmp2b[:, :]),
                        reads=("tmp2b",), writes=("SP16",))
                    kb.op("vector", lambda e, h16=h16: e.scalar_tensor_tensor(
                        Glo[:, h16 * TT:(h16 + 1) * TT],
                        Ghi[:, h16 * TT:(h16 + 1) * TT], -1.0, tmp2b[:, :],
                        ALU.mult, ALU.add),
                        reads=("tmp2b", "SP16"), writes=("SP16",))
                for dc in range(NC):
                    for q in range(2):
                        qi = q % 2
                        strip3(qi, f"pwmlp{dc}_{half}_{q}",
                               wmlpd[dc][:, half * 16 + q * 8:
                                         half * 16 + (q + 1) * 8, :],
                               pwmlp[dc, half, q])
                        mm3(qi, wsp[qi][0], wsp[qi][1],
                            lambda kc, q=q: Ghi[:, (q * 8 + kc) * TT:
                                                (q * 8 + kc + 1) * TT],
                            lambda kc, q=q: Glo[:, (q * 8 + kc) * TT:
                                                (q * 8 + kc + 1) * TT],
                            NC, (f"wsp{qi}", "SP16"))
                        if half == 0 and q == 0:
                            kb.op("vector", lambda e, dc=dc, b=qi,
                                  a2wkv=a2wkv, a2x1=a2x1:
                                  e.scalar_tensor_tensor(
                                plane(a2wkv, dc), ps[b][:, :],
                                mls[:, dc:dc + 1], plane(a2x1, dc),
                                ALU.mult, ALU.add),
                                reads=(f"ps{qi}", "vecs", "P2"),
                                writes=("P1",))
                        else:
                            kb.op("vector", lambda e, dc=dc, b=qi,
                                  a2wkv=a2wkv: e.scalar_tensor_tensor(
                                plane(a2wkv, dc), ps[b][:, :],
                                mls[:, dc:dc + 1], plane(a2wkv, dc),
                                ALU.mult, ALU.add),
                                reads=(f"ps{qi}", "vecs", "P1"),
                                writes=("P1",))
                        if half == 1 and q == 1:
                            kb.op("vector", lambda e, dc=dc, a2x=a2x,
                                  a2wkv=a2wkv: e.tensor_add(
                                plane(a2x, dc), plane(a2wkv, dc),
                                plane(a2x, dc)),
                                reads=("P1", "P0"), writes=("P0",))
                            kb.op("tensor", lambda t_, dc=dc, a2x=a2x:
                                  t_.matmul(
                                psStat, onesK[:, :], plane(a2x, dc),
                                start=(dc == 0), stop=(dc == NC - 1)),
                                reads=("P0", "onesK"), writes=("psRow",))
                            bufs = (tmpb, tmp2b)
                            names = ("tmpb", "tmp2b")
                            kb.op("scalar", lambda e, b=bufs[dc % 2],
                                  a2x=a2x, dc=dc: e.activation(
                                b[:, :], plane(a2x, dc), AF.Square),
                                reads=("P0",), writes=(names[dc % 2],))
                            kb.op("tensor", lambda t_, dc=dc,
                                  b=bufs[dc % 2]: t_.matmul(
                                psT[0:1, 0:TT], onesK[:, :], b[:, :],
                                start=(dc == 0), stop=(dc == NC - 1)),
                                reads=(names[dc % 2], "onesK"),
                                writes=("psT",))
            ln_finish(a2x, "P0", a2r, "P3")
            store(dxa[:, :, sl], a2r[:, :].rearrange("p (c j) -> p c j", c=NC),
                  ("P3",), (f"dxa{t}",), sem="stXa")
            if debug and t == 0:
                store(dbg["dbg_xa"], a2r[:, :].rearrange(
                    "p (c j) -> p c j", c=NC), ("P3",))

            def emit_router(t=t, a2r=a2r):
                fns = [lambda t_, kc=kc, a2r=a2r: t_.matmul(
                    ps[5][0:E, :], rwsb[:, kc * E:(kc + 1) * E],
                    plane(a2r, kc),
                    start=(kc == 0), stop=(kc == NC - 1))
                    for kc in range(NC)]
                kb.op("tensor", fns, reads=("rw", "P3"), writes=("ps5",))
                kb.op("vector", lambda e: e.tensor_scalar(
                    logits[:, :], ps[5][0:E, :], rbsb[:, 0:1], None, ALU.add),
                    reads=("ps5", "rb"), writes=("logits",))
                fns = []
                for jj in range(4):
                    fns.append(lambda t_, jj=jj: t_.matmul(
                        psT[:, jj * 8:(jj + 1) * 8],
                        logits[:, jj * 128:(jj + 1) * 128], ident[0:E, 0:E],
                        is_transpose=True, start=(jj == 0), stop=(jj == 3)))
                kb.op("tensor", fns, reads=("logits", "ident"),
                      writes=("psT",))
                kb.op("vector", lambda e, t=t: e.tensor_copy(
                    ltm[:, t * 32:(t + 1) * 32], psT[:, 0:32]),
                    reads=("psT",), writes=("ltm",))

            routers.append(emit_router)

        while routers:
            routers.pop(0)()

        if debug:
            store(dbg["dbg_ltm"], ltm[:, :], ("ltm",))
        kb.op("vector", lambda e: e.reduce_max(rowm[:, :], v3(ltm[:, :]),
                                               axis=AX.X),
              reads=("ltm",), writes=("rowm",))
        kb.op("vector", lambda e: e.tensor_tensor(
            v3(uu[:, :]), v3(ltm[:, :]),
            rowm[:, :].unsqueeze(2).broadcast_to((128, 16, 8)),
            ALU.subtract), reads=("ltm", "rowm"), writes=("uu",))
        kb.op("scalar", lambda e: e.activation(uu[:, :], uu[:, :], AF.Exp),
              reads=("uu",), writes=("uu",))
        kb.op("vector", lambda e: e.reduce_max(rowm[:, :], v3(uu[:, :]),
                                               axis=AX.X),
              reads=("uu",), writes=("rowm",))
        kb.op("vector", lambda e: e.tensor_tensor(
            v3(eq1[:, :]), v3(uu[:, :]),
            rowm[:, :].unsqueeze(2).broadcast_to((128, 16, 8)),
            ALU.is_equal), reads=("uu", "rowm"), writes=("eq1",))
        kb.op("vector", lambda e: e.tensor_mul(eq2[:, :], uu[:, :], eq1[:, :]),
              reads=("uu", "eq1"), writes=("eq2",))
        kb.op("vector", lambda e: e.tensor_sub(eq2[:, :], uu[:, :], eq2[:, :]),
              reads=("uu", "eq2"), writes=("eq2",))
        kb.op("vector", lambda e: e.reduce_max(rowm2[:, :], v3(eq2[:, :]),
                                               axis=AX.X),
              reads=("eq2",), writes=("rowm2",))
        kb.op("vector", lambda e: e.tensor_tensor(
            v3(eq2[:, :]), v3(eq2[:, :]),
            rowm2[:, :].unsqueeze(2).broadcast_to((128, 16, 8)),
            ALU.is_equal), reads=("eq2", "rowm2"), writes=("eq2",))
        kb.op("vector", lambda e: e.tensor_add(roww[:, 0:16], rowm[:, :],
                                               rowm2[:, :]),
              reads=("rowm", "rowm2"), writes=("roww",))
        kb.op("vector", lambda e: e.reciprocal(roww[:, 0:16], roww[:, 0:16]),
              reads=("roww",), writes=("roww",))
        kb.op("vector", lambda e: e.tensor_mul(roww[:, 16:32], rowm[:, :],
                                               roww[:, 0:16]),
              reads=("rowm", "roww"), writes=("roww",))
        kb.op("vector", lambda e: e.tensor_mul(roww[:, 32:48], rowm2[:, :],
                                               roww[:, 0:16]),
              reads=("rowm2", "roww"), writes=("roww",))
        a01i = a01[:, :].rearrange("p (a kb) -> p a kb", a=16)
        kb.op("vector", lambda e: e.tensor_tensor(
            a01i[:, :, 0:8], v3(eq1[:, :]),
            roww[:, 16:32].unsqueeze(2).broadcast_to((128, 16, 8)),
            ALU.mult), reads=("eq1", "roww"), writes=("a01",))
        kb.op("vector", lambda e: e.tensor_tensor(
            a01i[:, :, 8:16], v3(eq2[:, :]),
            roww[:, 32:48].unsqueeze(2).broadcast_to((128, 16, 8)),
            ALU.mult), reads=("eq2", "roww"), writes=("a01",))
        kb.op("vector", lambda e: e.tensor_tensor(
            v3(masktm[:, :]), a01i[:, :, 0:8], a01i[:, :, 8:16],
            ALU.add), reads=("a01",), writes=("masktm",))
        fns = [lambda t_, jj=jj: t_.matmul(
            psT[0:16, 0:1], a01[:, jj * 16:(jj + 1) * 16], onesK[:, :],
            start=(jj == 0), stop=(jj == 15)) for jj in range(16)]
        kb.op("tensor", fns, reads=("a01", "onesK"), writes=("psT",))
        kb.op("vector", lambda e: e.tensor_copy(tcol[:, 0:1], psT[0:16, 0:1]),
              reads=("psT",), writes=("tcol",))
        kb.op("tensor", lambda t_: t_.matmul(
            psT[0:1, 0:16], tcol[:, 0:1], ident[0:16, 0:16],
            is_transpose=True, start=True, stop=True),
            reads=("tcol", "ident"), writes=("psT",))
        kb.op("scalar", lambda e: e.copy(tpad[:, 0:16], psT[0:1, 0:16]),
              reads=("psT",), writes=("tpad",))
        store(cc2i[0:1, :], tpad[0:1, :], ("tpad",), ("cc2i",), sem="stCC2")
        kb.op("gpsimd", lambda e: e.collective_compute(
            "AllGather", ALU.bypass, replica_groups=[list(range(8))],
            ins=[cc2i.opt()], outs=[cc2o.opt()]),
            reads=("cc2i",), writes=("cc2o",), sem="sCC", inc=1)

        for grp in range(4):
            fns = []
            for j4 in range(4):
                jj = grp * 4 + j4
                fns.append(lambda t_, jj=jj, j4=j4: t_.matmul(
                    psT[0:E, j4 * 128:(j4 + 1) * 128],
                    masktm[:, jj * 8:(jj + 1) * 8], ident[:, :],
                    is_transpose=True, start=(j4 == 0), stop=(j4 == 3)))
            kb.op("tensor", fns, reads=("masktm", "ident"), writes=("psT",))
            kb.op("vector", lambda e, grp=grp: e.tensor_copy(
                maskem[:, grp * TT:(grp + 1) * TT], psT[0:E, 0:TT]),
                reads=("psT",), writes=("maskem",))

        def pass_b_tile(t):
            sl = slice(t * TT, (t + 1) * TT)
            bxa, bmacc, by = P[0], P[1], P[2]
            load(bxa[:, :].rearrange("p (c j) -> p c j", c=NC),
                 dxa[:, :, sl], ("P0",), reads=(f"dxa{t}",), sem="lP0")
            for c in range(NC):
                split16(plane(bxa, c), plane(sphi, c), plane(splo, c),
                        ("P0",), "sp")
            for e_ in range(E):
                kb.op("tensor", lambda t_, e_=e_, sl=sl: t_.matmul(
                    ps[5][:, :], esel[:, e_ * 128:(e_ + 1) * 128],
                    maskem[0:E, sl], start=True, stop=True),
                    reads=("maskem", "esel"), writes=("ps5",))
                kb.op("scalar", lambda e: e.copy(maskb[:, :], ps[5][:, :]),
                      reads=("ps5",), writes=("maskb",))
                for hk in range(NC):
                    strip3(hk % 2, f"pw1_{e_}_{hk}", w1d[e_, hk],
                           pw1[e_, hk])
                    mm3(hk % 4, wsp[hk % 2][0], wsp[hk % 2][1],
                        lambda kc: plane(sphi, kc), lambda kc: plane(splo, kc),
                        NC, (f"wsp{hk % 2}", "sp"))
                    kb.op("vector", lambda e, b=hk % 4: e.tensor_mul(
                        tmpb[:, :], ps[b][:, :], maskb[:, :]),
                        reads=(f"ps{hk % 4}", "maskb"), writes=("tmpb",))
                    kb.op("scalar", lambda e, e_=e_, hk=hk: e.activation(
                        tmpb[:, :], tmpb[:, :], AF.Gelu,
                        bias=b1v[:, e_ * 8 + hk:e_ * 8 + hk + 1]),
                        reads=("tmpb", "b1v"), writes=("tmpb",))
                    kb.op("vector", lambda e, hk=hk: e.tensor_mul(
                        h2hi[:, hk * TT:(hk + 1) * TT], tmpb[:, :],
                        maskb[:, :]),
                        reads=("tmpb", "maskb"), writes=("SP16",))
                    kb.op("vector", lambda e: e.tensor_mul(
                        tmp2b[:, :], tmpb[:, :], maskb[:, :]),
                        reads=("tmpb", "maskb"), writes=("tmp2b",))
                    kb.op("vector", lambda e, hk=hk: e.scalar_tensor_tensor(
                        h2lo[:, hk * TT:(hk + 1) * TT],
                        h2hi[:, hk * TT:(hk + 1) * TT], -1.0, tmp2b[:, :],
                        ALU.mult, ALU.add),
                        reads=("tmp2b", "SP16"), writes=("SP16",))
                for dc in range(NC):
                    strip3(dc % 2, f"pw2_{e_}_{dc}", w2d[e_, dc],
                           pw2[e_, dc], bf=True)
                    mm3(dc % 4, wspb[dc % 2][0], wspb[dc % 2][1],
                        lambda kc: h2hi[:, kc * TT:(kc + 1) * TT],
                        lambda kc: h2lo[:, kc * TT:(kc + 1) * TT],
                        NC, (f"wsp{dc % 2}", "SP16"))
                    if e_ == 0:
                        kb.op("vector", lambda e, dc=dc, b=dc % 4,
                              bmacc=bmacc: e.tensor_copy(
                              plane(bmacc, dc), ps[b][:, :]),
                              reads=(f"ps{dc % 4}",), writes=("P1",))
                    else:
                        kb.op("vector", lambda e, dc=dc, b=dc % 4,
                              bmacc=bmacc: e.tensor_add(
                              plane(bmacc, dc), ps[b][:, :],
                              plane(bmacc, dc)),
                              reads=(f"ps{dc % 4}", "P1"), writes=("P1",))
            for dc in range(NC):
                kb.op("tensor", lambda t_, dc=dc, sl=sl: t_.matmul(
                    ps[dc % 4][:, :], b2r[:, dc * 128:(dc + 1) * 128],
                    maskem[0:E, sl], start=True, stop=True),
                    reads=("b2r", "maskem"), writes=(f"ps{dc % 4}",))
                kb.op("vector", lambda e, dc=dc, b=dc % 4, bmacc=bmacc:
                      e.tensor_add(plane(bmacc, dc), ps[b][:, :],
                                   plane(bmacc, dc)),
                      reads=(f"ps{dc % 4}", "P1"), writes=("P1",))
            if debug and t == 0:
                store(dbg["dbg_macc"], bmacc[:, :].rearrange(
                    "p (c j) -> p c j", c=NC), ("P1",))
            for c in range(NC):
                kb.op("vector", lambda e, c=c, bmacc=bmacc, bxa=bxa:
                      e.tensor_add(
                    plane(bmacc, c), plane(bmacc, c), plane(bxa, c)),
                    reads=("P1", "P0"), writes=("P1",))
            layernorm(bmacc, "P1", by, "P2")
            store(yd[:, :, sl], by[:, :].rearrange("p (c j) -> p c j", c=NC),
                  ("P2",), (f"ydone{t}",), sem="stY")

        for t in (1, 2, 3):
            pass_b_tile(t)

        load(trow[:, :], cc2o.transpose([1, 0])[0:16, :], ("trow",),
             reads=("cc2o",), slow=True, sem="lT")
        kb.op("vector", lambda e: e.reduce_sum(tcol[:, 0:1], trow[:, :],
                                               axis=AX.X),
              reads=("trow",), writes=("tcol",))
        kb.op("gpsimd", lambda e: e.dma_start(
            out=fixem[:, :], in_=tcol[:, 0:1],
            allow_slow_non_contiguous=True),
            reads=("tcol",), writes=("fixem",), sem="lFix")
        kb.op("vector", lambda e: e.tensor_scalar(
            fixem[:, :], fixem[:, :], fixg[:, 0:1], None, ALU.mult),
            reads=("fixem", "fixg"), writes=("fixem",))
        kb.op("vector", lambda e: e.tensor_tensor(
            maskem[0:2, 0:8], maskem[0:2, 0:8], fixem[:, :], ALU.add),
            reads=("maskem", "fixem"), writes=("maskem",))
        kb.op("vector", lambda e: e.tensor_scalar_min(
            maskem[0:2, 0:8], maskem[0:2, 0:8], 2560.0),
            reads=("maskem",), writes=("maskem",))
        if debug:
            store(dbg["dbg_maskem"], maskem[:, :], ("maskem",))

        pass_b_tile(0)

        vy = kb.cnt["stY"]
        kb.streams["gpsimd"].append(
            lambda e, v=vy: e.wait_ge(sems["stY"], v))
        if debug:
            vd = kb.cnt["stDbg"]
            kb.streams["gpsimd"].append(
                lambda e, v=vd: e.wait_ge(sems["stDbg"], v))

        with nc.Block() as block:
            @block.sync
            def _(eng):
                for fn in kb.streams["sync"]:
                    fn(eng)

            @block.gpsimd
            def _(eng):
                for fn in kb.streams["gpsimd"]:
                    fn(eng)

            @block.tensor
            def _(eng):
                for fn in kb.streams["tensor"]:
                    fn(eng)

            @block.scalar
            def _(eng):
                for fn in kb.streams["scalar"]:
                    fn(eng)

            @block.vector
            def _(eng):
                for fn in kb.streams["vector"]:
                    fn(eng)

    return nc


def _tile4(wt, nk, nm):
    return np.ascontiguousarray(
        wt.reshape(nk, 128, nm, 128).transpose(2, 1, 0, 3))


_CACHE = {}
_LAST_RES = None


def kernel(**inputs):
    f32 = np.float32
    x = np.asarray(inputs["L0_attn_x"], f32)
    wk = _tile4(np.ascontiguousarray(inputs["L0_attn_Wk"].T.astype(f32)), NC, NC)
    wv = _tile4(np.ascontiguousarray(inputs["L0_attn_Wv"].T.astype(f32)), NC, NC)
    wr = _tile4(np.ascontiguousarray(inputs["L0_attn_Wr"].T.astype(f32)), NC, NC)
    wo = _tile4(np.ascontiguousarray(inputs["L0_attn_Wo"].T.astype(f32)), NC, NC)
    wfc = _tile4(np.ascontiguousarray(inputs["L0_attn_Wfc"].T.astype(f32)), NC, NHC)
    wmlp = _tile4(np.ascontiguousarray(inputs["L0_attn_Wmlp"].T.astype(f32)), NHC, NC)
    w1 = np.ascontiguousarray(
        np.asarray(inputs["L0_ffn_w1"], f32).reshape(E, NC, 128, NC, 128)
        .transpose(0, 3, 2, 1, 4))
    w2 = np.ascontiguousarray(
        np.asarray(inputs["L0_ffn_w2"], f32).reshape(E, NC, 128, NC, 128)
        .transpose(0, 3, 2, 1, 4))
    rw = np.ascontiguousarray(
        np.asarray(inputs["L0_ffn_router_w"], f32).reshape(NC, 128, E)
        .transpose(1, 0, 2))
    b2r = np.ascontiguousarray(np.asarray(inputs["L0_ffn_b2"], f32))
    b1 = np.asarray(inputs["L0_ffn_b1"], f32)
    b1v = np.ascontiguousarray(
        np.concatenate([b1[e].reshape(8, 128).T for e in range(E)], axis=1))
    packv = lambda v: np.asarray(v, f32).reshape(8, 128).T
    vecs = np.ascontiguousarray(np.concatenate(
        [packv(inputs["L0_attn_time_decay"]),
         packv(inputs["L0_attn_time_first"]),
         packv(inputs["L0_attn_mixer_scale"]),
         packv(inputs["L0_attn_mlp_scale"])], axis=1))
    rb = np.ascontiguousarray(
        np.asarray(inputs["L0_ffn_router_b"], f32).reshape(E, 1))
    ident = np.eye(128, dtype=f32)
    esel = np.zeros((E, E * 128), f32)
    for e in range(E):
        esel[e, e * 128:(e + 1) * 128] = 1.0

    if _CACHE.get("dbg") != DEBUG:
        _CACHE["nc"] = build_program(debug=DEBUG)
        _CACHE["dbg"] = DEBUG
    nc = _CACHE["nc"]

    in_maps = []
    for core in range(8):
        b, h = core // 2, core % 2
        x_fm = np.ascontiguousarray(x[b, h * TPC:(h + 1) * TPC, :].T)
        xcore = np.ascontiguousarray(
            x_fm.reshape(NC, 128, TPC).transpose(1, 0, 2))
        sselrow = np.zeros(E, f32)
        if h == 1:
            sselrow[2 * b] = 1.0
        ssel = np.ascontiguousarray(np.tile(sselrow, (128, 1)))
        fixgv = np.full((2, 1), 1.0 if core == 0 else 0.0, f32)
        in_maps.append({
            "x": xcore, "wk": wk, "wv": wv, "wr": wr, "wo": wo,
            "wfc": wfc, "wmlp": wmlp, "w1": w1, "w2": w2, "rw": rw,
            "b2r": b2r, "b1v": b1v, "vecs": vecs, "rb": rb,
            "ident": ident, "esel": esel, "ssel": ssel, "fixg": fixgv,
        })

    res = run_bass_kernel_spmd(nc, in_maps, core_ids=list(range(8)))
    _CACHE["res"] = res
    global _LAST_RES
    _LAST_RES = res

    out = np.empty((4, 4096, D), f32)
    for core in range(8):
        b, h = core // 2, core % 2
        ycore = res.results[core]["y"]
        y_fm = ycore.transpose(1, 0, 2).reshape(D, TPC)
        out[b, h * TPC:(h + 1) * TPC, :] = y_fm.T
    return out


# version 06 (final)
# speedup vs baseline: 1.0152x; 1.0152x over previous
import numpy as np
from contextlib import ExitStack

import concourse.bass as bass
import concourse.mybir as mybir
from concourse.bass_utils import run_bass_kernel_spmd

F32 = mybir.dt.float32
F16 = mybir.dt.float16
BF16 = mybir.dt.bfloat16
AF = mybir.ActivationFunctionType
ALU = mybir.AluOpType
AX = mybir.AxisListType

D = 1024
E = 8
TPC = 2048
NT = 4
TT = 512
NC = 8
NHC = 32

ENGINES = ("sync", "gpsimd", "tensor", "scalar", "vector")
ESEM = {"sync": "sSYE", "gpsimd": "sGE", "tensor": "sTE", "scalar": "sAC",
        "vector": "sVE"}
DMA_SEMS = frozenset(
    ["lConst", "lP0", "lP1", "lP2", "lSA", "lSB0", "lSB1", "lS0", "lT",
     "lFix", "stN", "stW", "stXa", "stY", "stCC1", "stCC2", "stDbg",
     "stPW"])

DEBUG = False


class KB:

    def __init__(self, sems):
        self.sems = sems
        self.streams = {e: [] for e in ENGINES}
        self.cnt = {s: 0 for s in sems}
        self.wm = {e: {s: 0 for s in sems} for e in ENGINES}
        self.buf = {}

    def _wait(self, eng, tag):
        sem, val = tag
        if sem in DMA_SEMS:
            val = self.cnt[sem]
        if self.wm[eng][sem] >= val:
            return
        self.wm[eng][sem] = val
        h = self.sems[sem]
        self.streams[eng].append(lambda e, h=h, v=val: e.wait_ge(h, v))

    def alias(self, new, oldname):
        st = self.buf.get(oldname)
        if st:
            self.buf[new] = {"prod": st["prod"], "readers": list(st["readers"])}

    def op(self, eng, fns, reads=(), writes=(), inc=None, sem=None):
        if callable(fns):
            fns = [fns]
        for b in reads:
            st = self.buf.get(b)
            if st and st["prod"]:
                self._wait(eng, st["prod"])
        for b in writes:
            st = self.buf.get(b)
            if st:
                if st["prod"]:
                    self._wait(eng, st["prod"])
                for t in st["readers"]:
                    self._wait(eng, t)
        semname = sem or ESEM[eng]
        if inc is None:
            inc = 16 if semname in DMA_SEMS else 1
        self.cnt[semname] += inc
        tag = (semname, self.cnt[semname])
        h = self.sems[semname]
        for fn in fns[:-1]:
            self.streams[eng].append(fn)
        self.streams[eng].append(
            lambda e, fn=fns[-1], h=h, n=inc: fn(e).then_inc(h, n))
        for b in reads:
            self.buf.setdefault(b, {"prod": None, "readers": []})[
                "readers"].append(tag)
        for b in writes:
            self.buf[b] = {"prod": tag, "readers": []}
        return tag


def build_program(debug=False):
    nc = bass.Bass("TRN2", target_bir_lowering=False)

    xd = nc.dram_tensor("x", [128, NC, TPC], F32, kind="ExternalInput").ap()
    wkd = nc.dram_tensor("wk", [NC, 128, NC, 128], F32, kind="ExternalInput").ap()
    wvd = nc.dram_tensor("wv", [NC, 128, NC, 128], F32, kind="ExternalInput").ap()
    wrd = nc.dram_tensor("wr", [NC, 128, NC, 128], F32, kind="ExternalInput").ap()
    wod = nc.dram_tensor("wo", [NC, 128, NC, 128], F32, kind="ExternalInput").ap()
    wfcd = nc.dram_tensor("wfc", [NHC, 128, NC, 128], F32, kind="ExternalInput").ap()
    wmlpd = nc.dram_tensor("wmlp", [NC, 128, NHC, 128], F32, kind="ExternalInput").ap()
    w1d = nc.dram_tensor("w1", [E, NC, 128, NC, 128], F32, kind="ExternalInput").ap()
    w2d = nc.dram_tensor("w2", [E, NC, 128, NC, 128], F32, kind="ExternalInput").ap()
    rwd = nc.dram_tensor("rw", [128, NC, E], F32, kind="ExternalInput").ap()
    b2rd = nc.dram_tensor("b2r", [E, D], F32, kind="ExternalInput").ap()
    b1vd = nc.dram_tensor("b1v", [128, 64], F32, kind="ExternalInput").ap()
    vecsd = nc.dram_tensor("vecs", [128, 32], F32, kind="ExternalInput").ap()
    rbd = nc.dram_tensor("rb", [E, 1], F32, kind="ExternalInput").ap()
    identd = nc.dram_tensor("ident", [128, 128], F32, kind="ExternalInput").ap()
    eseld = nc.dram_tensor("esel", [E, E * 128], F32, kind="ExternalInput").ap()
    sseld = nc.dram_tensor("ssel", [128, E], F32, kind="ExternalInput").ap()
    fixgd = nc.dram_tensor("fixg", [2, 1], F32, kind="ExternalInput").ap()

    yd = nc.dram_tensor("y", [128, NC, TPC], F32, kind="ExternalOutput").ap()

    pwk = nc.dram_tensor("pwk", [NC, 128, 2048], F16).ap()
    pwv = nc.dram_tensor("pwv", [NC, 128, 2048], F16).ap()
    pwr = nc.dram_tensor("pwr", [NC, 128, 2048], F16).ap()
    pwo = nc.dram_tensor("pwo", [NC, 128, 2048], F16).ap()
    pwfc = nc.dram_tensor("pwfc", [NHC, 128, 2048], F16).ap()
    pwmlp = nc.dram_tensor("pwmlp", [NC, 2, 2, 128, 2048], F16).ap()
    pw1 = nc.dram_tensor("pw1", [E, NC, 128, 2048], F16).ap()
    pw2 = nc.dram_tensor("pw2", [E, NC, 128, 2048], BF16).ap()
    dnormh = nc.dram_tensor("dnormh", [128, NC, TPC], F16).ap()
    dnorml = nc.dram_tensor("dnorml", [128, NC, TPC], F16).ap()
    dwkv = nc.dram_tensor("dwkv", [128, NC, TPC], F32).ap()
    dxa = nc.dram_tensor("dxa", [128, NC, TPC], F32).ap()
    cc1i = nc.dram_tensor("cc1i", [1, D], F32).ap()
    cc1o = nc.dram_tensor("cc1o", [NC, D], F32).ap()
    cc2i = nc.dram_tensor("cc2i", [1, D], F32).ap()
    cc2o = nc.dram_tensor("cc2o", [NC, D], F32).ap()

    dbg = {}
    if debug:
        for nm in ("dbg_normed", "dbg_wkv", "dbg_m", "dbg_x1", "dbg_xa",
                   "dbg_macc", "dbg_k", "dbg_kv"):
            dbg[nm] = nc.dram_tensor(nm, [128, NC, TT], F32,
                                     kind="ExternalOutput").ap()
        dbg["dbg_maskem"] = nc.dram_tensor("dbg_maskem", [E, TPC], F32,
                                           kind="ExternalOutput").ap()
        dbg["dbg_ltm"] = nc.dram_tensor("dbg_ltm", [128, 128], F32,
                                        kind="ExternalOutput").ap()
        dbg["dbg_scan"] = nc.dram_tensor("dbg_scan", [128, TT + 1], F32,
                                         kind="ExternalOutput").ap()
        dbg["dbg_wkv0"] = nc.dram_tensor("dbg_wkv0", [128, TT], F32,
                                         kind="ExternalOutput").ap()

    with ExitStack() as ctx:
        sb = lambda name, shape: ctx.enter_context(
            nc.sbuf_tensor(name, shape, F32))
        P = [sb(f"p{i}", [128, NC * TT]) for i in range(4)]
        sbh = lambda name, shape: ctx.enter_context(
            nc.sbuf_tensor(name, shape, F16))
        SP16 = sbh("sp16", [128, 2 * 16 * TT])
        Ghi = SP16[:, 0:16 * TT]
        Glo = SP16[:, 16 * TT:32 * TT]
        h2banks = [
            (SP16[:, 0:NC * TT].bitcast(BF16),
             SP16[:, NC * TT:2 * NC * TT].bitcast(BF16)),
            (SP16[:, 2 * NC * TT:3 * NC * TT].bitcast(BF16),
             SP16[:, 3 * NC * TT:4 * NC * TT].bitcast(BF16)),
        ]
        sphi = sbh("sphi", [128, NC * TT])
        splo = sbh("splo", [128, NC * TT])
        wsp = [(sbh("wsp0h", [128, 1024]), sbh("wsp0l", [128, 1024])),
               (sbh("wsp1h", [128, 1024]), sbh("wsp1l", [128, 1024]))]
        wspb = [(wsp[0][0].bitcast(BF16), wsp[0][1].bitcast(BF16)),
                (wsp[1][0].bitcast(BF16), wsp[1][1].bitcast(BF16))]
        stripB = [sb("stripB0", [128, 1024]), sb("stripB1", [128, 1024])]
        ktmp = sb("ktmpS", [128, TT])
        rtmp = sb("rtmpS", [128, TT])
        ident = sb("identS", [128, 128])
        esel = sb("eselS", [E, E * 128])
        ones1 = sb("ones1", [1, 128])
        onesK = sb("onesK", [128, 1])
        vecs = sb("vecsS", [128, 32])
        ew = sb("ewS", [128, NC])
        eu = sb("euS", [128, NC])
        ne1 = sb("ne1S", [128, NC])
        sstate = sb("sstateS", [128, NC])
        s0 = sb("s0S", [128, NC])
        s0raw = sb("s0rawS", [128, NC * NC])
        ssel = sb("sselS", [128, E])
        scanbuf = sb("scanbufS", [128, TT + 1])
        rampb = sb("rampS", [128, TT])
        tmpb = sb("tmpS", [128, TT])
        tmp2b = sb("tmp2S", [128, TT])
        rwsb = sb("rwS", [128, NC * E])
        rbsb = sb("rbS", [E, 1])
        b1v = sb("b1vS", [128, 64])
        b2r = sb("b2rS", [E, D])
        logits = sb("logitsS", [E, TT])
        ltm = sb("ltmS", [128, 128])
        uu = sb("uuS", [128, 128])
        eq1 = sb("eq1S", [128, 128])
        eq2 = sb("eq2S", [128, 128])
        a01 = sb("a01S", [128, 256])
        masktm = sb("masktmS", [128, 128])
        maskem = sb("maskemS", [E, TPC])
        maskb = sb("maskbS", [128, TT])
        rowm = sb("rowmS", [128, 16])
        rowm2 = sb("rowm2S", [128, 16])
        roww = sb("rowwS", [128, 48])
        statrow = sb("statrowS", [1, 4 * TT])
        bcA = sb("bcAS", [128, TT])
        bcB = sb("bcBS", [128, TT])
        trow = sb("trowS", [16, NC])
        tcol = sb("tcolS", [16, 1])
        tpad = sb("tpadS", [1, D])
        fixem = sb("fixemS", [2, E])
        fixg = sb("fixgS", [2, 1])
        epsr = sb("epsrS", [1, 2])

        ps = [ctx.enter_context(nc.psum_tensor(f"ps{i}", [128, TT], F32))
              for i in range(6)]
        psRow = ctx.enter_context(nc.psum_tensor("psRow", [128, TT], F32))
        psT = ctx.enter_context(nc.psum_tensor("psT", [128, TT], F32))
        psStat = psRow[0:1, :]

        semnames = (["sTE", "sAC", "sVE", "sGE", "sCC", "sSYE"]
                    + sorted(DMA_SEMS))
        sems = {s: ctx.enter_context(nc.semaphore(s)) for s in semnames}
        kb = KB(sems)

        def plane(p, c):
            return p[:, c * TT:(c + 1) * TT]

        def v3(ap, a=16):
            return ap.rearrange("p (a b) -> p a b", a=a)

        def load(dst_ap, src_ap, writes, reads=(), slow=False, sem="lConst"):
            kb.op("sync", lambda e, d=dst_ap, s=src_ap, sl_=slow: e.dma_start(
                out=d, in_=s, allow_slow_non_contiguous=sl_),
                reads=reads, writes=writes, sem=sem)

        def store(dst_ap, src_ap, reads, writes=(), slow=False, sem="stDbg"):
            kb.op("gpsimd", lambda e, d=dst_ap, s=src_ap, sl_=slow: e.dma_start(
                out=d, in_=s, allow_slow_non_contiguous=sl_),
                reads=reads, writes=writes, sem=sem)

        def strip_load(buf_i, src):
            sbuf = stripB[buf_i]
            load(sbuf[:, :].rearrange("p (c m) -> p c m", c=NC), src,
                 (f"stripB{buf_i}",), sem=f"lSB{buf_i}")

        def mm_chain(bank_i, strip, rhs_of_kc, nk, reads, extra=()):
            bank = ps[bank_i]
            total = nk + len(extra)
            fns = []
            for kc in range(nk):
                fns.append(lambda t, kc=kc, bank=bank, strip=strip: t.matmul(
                    bank[:, :], strip[:, kc * 128:(kc + 1) * 128],
                    rhs_of_kc(kc), start=(kc == 0),
                    stop=(kc == total - 1)))
            for i, (lh, rh) in enumerate(extra):
                fns.append(lambda t, lh=lh, rh=rh, i=i, bank=bank: t.matmul(
                    bank[:, :], lh, rh, start=False,
                    stop=(nk + i == total - 1)))
            return kb.op("tensor", fns, reads=reads, writes=(f"ps{bank_i}",))

        def split16(src_ap, hi_ap, lo_ap, reads, wname):
            kb.op("scalar", lambda e, s=src_ap, h=hi_ap: e.copy(h, s),
                  reads=reads, writes=(wname,))
            kb.op("vector", lambda e, s=src_ap, h=hi_ap, l=lo_ap:
                  e.scalar_tensor_tensor(l, h, -1.0, s, ALU.mult, ALU.add),
                  reads=reads + (wname,), writes=(wname,))

        def split_strip(i, bf=False):
            pair = wspb[i] if bf else wsp[i]
            split16(stripB[i][:, :], pair[0][:, :], pair[1][:, :],
                    (f"stripB{i}",), f"wsp{i}")

        presplit_done = set()

        def strip3(i, key, fp32_src, pre, bf=False):
            pair = wspb[i] if bf else wsp[i]
            if key not in presplit_done:
                presplit_done.add(key)
                strip_load(i, fp32_src)
                split_strip(i, bf=bf)
                store(pre[:, 0:1024], pair[0][:, :], (f"wsp{i}",), (key,),
                      sem="stPW")
                store(pre[:, 1024:2048], pair[1][:, :], (f"wsp{i}",), (key,),
                      sem="stPW")
            else:
                load(pair[0][:, :], pre[:, 0:1024], (f"wsp{i}",),
                     reads=(key,), sem=f"lSB{i}")
                load(pair[1][:, :], pre[:, 1024:2048], (f"wsp{i}",),
                     reads=(key,), sem=f"lSB{i}")

        def mm3(bank_i, whi, wlo, xhi_of, xlo_of, nk, reads, extra=()):
            bank = ps[bank_i]
            total = 3 * nk + len(extra)
            fns = []
            n = 0
            for kc in range(nk):
                for (wv_, xv) in ((whi, xhi_of(kc)), (whi, xlo_of(kc)),
                                  (wlo, xhi_of(kc))):
                    fns.append(lambda t, w=wv_, x=xv, kc=kc, n=n, bank=bank:
                               t.matmul(bank[:, :],
                                        w[:, kc * 128:(kc + 1) * 128], x,
                                        start=(n == 0),
                                        stop=(n == total - 1)))
                    n += 1
            for (lh, rh) in extra:
                fns.append(lambda t, lh=lh, rh=rh, n=n, bank=bank: t.matmul(
                    bank[:, :], lh, rh, start=False, stop=(n == total - 1)))
                n += 1
            return kb.op("tensor", fns, reads=reads, writes=(f"ps{bank_i}",))

        load(ident[:, :], identd, ("ident",))
        load(esel[:, :], eseld, ("esel",))
        load(vecs[:, :], vecsd, ("vecs",))
        load(rwsb[:, :].rearrange("p (c e) -> p c e", c=NC), rwd, ("rw",))
        load(rbsb[:, :], rbd, ("rb",))
        load(b1v[:, :], b1vd, ("b1v",))
        load(b2r[:, :], b2rd, ("b2r",))
        load(ssel[:, :], sseld, ("ssel",))
        load(fixg[:, :], fixgd, ("fixg",))
        kb.op("vector", lambda e: e.memset(ones1[:, :], 1.0), writes=("ones1",))
        kb.op("vector", lambda e: e.memset(onesK[:, :], 1.0), writes=("onesK",))
        kb.op("vector", lambda e: e.memset(sstate[:, :], 0.0),
              writes=tuple(f"sst{c}" for c in range(NC)))
        kb.op("vector", lambda e: e.memset(tpad[:, :], 0.0), writes=("tpad",))
        kb.op("vector", lambda e: e.memset(epsr[:, 0:1], 1e-6), writes=("epsr",))
        kb.op("vector", lambda e: e.memset(epsr[:, 1:2], 1e-5), writes=("epsr",))
        kb.op("scalar", lambda e: e.activation(ne1[:, :], vecs[:, 0:8], AF.Exp),
              reads=("vecs",), writes=("ne1",))
        kb.op("scalar", lambda e: e.activation(ew[:, :], ne1[:, :], AF.Exp,
                                               scale=-1.0),
              reads=("ne1",), writes=("ew",))
        kb.op("scalar", lambda e: e.activation(eu[:, :], vecs[:, 8:16], AF.Exp),
              reads=("vecs",), writes=("eu",))
        kb.op("vector", lambda e: e.tensor_scalar_mul(ne1[:, :], ne1[:, :],
                                                      -1.0),
              reads=("ne1",), writes=("ne1",))

        mxs = vecs[:, 16:24]
        mls = vecs[:, 24:32]

        def sq_chain(src_p, sname):
            bufs = (tmpb, tmp2b)
            names = ("tmpb", "tmp2b")
            for c in range(NC):
                kb.op("scalar", lambda e, c=c, b=bufs[c % 2]: e.activation(
                    b[:, :], plane(src_p, c), AF.Square),
                    reads=(sname,), writes=(names[c % 2],))
                kb.op("tensor", lambda t, c=c, b=bufs[c % 2]: t.matmul(
                    psStat, onesK[:, :], b[:, :],
                    start=(c == 0), stop=(c == NC - 1)),
                    reads=(names[c % 2], "onesK"), writes=("psRow",))

        def rms_stats(src_p, sname):
            sq_chain(src_p, sname)
            rms_finish()

        def rms_finish():
            kb.op("vector", lambda e: e.tensor_scalar(
                statrow[:, 0:TT], psStat, 1.0 / D, 1e-6, ALU.mult, ALU.add),
                reads=("psRow",), writes=("statrow",))
            kb.op("scalar", lambda e: e.activation(
                statrow[:, 2 * TT:3 * TT], statrow[:, 0:TT], AF.Ln),
                reads=("statrow",), writes=("statrow",))
            kb.op("scalar", lambda e: e.activation(
                statrow[:, TT:2 * TT], statrow[:, 2 * TT:3 * TT], AF.Exp,
                scale=-0.5),
                reads=("statrow",), writes=("statrow",))
            for _nr in range(1):
                kb.op("vector", lambda e: e.tensor_mul(
                    statrow[:, 2 * TT:3 * TT], statrow[:, 0:TT],
                    statrow[:, TT:2 * TT]),
                    reads=("statrow",), writes=("statrow",))
                kb.op("vector", lambda e: e.tensor_mul(
                    statrow[:, 2 * TT:3 * TT], statrow[:, 2 * TT:3 * TT],
                    statrow[:, TT:2 * TT]),
                    reads=("statrow",), writes=("statrow",))
                kb.op("vector", lambda e: e.tensor_scalar(
                    statrow[:, 2 * TT:3 * TT], statrow[:, 2 * TT:3 * TT],
                    -0.5, 1.5, ALU.mult, ALU.add),
                    reads=("statrow",), writes=("statrow",))
                kb.op("vector", lambda e: e.tensor_mul(
                    statrow[:, TT:2 * TT], statrow[:, TT:2 * TT],
                    statrow[:, 2 * TT:3 * TT]),
                    reads=("statrow",), writes=("statrow",))
            kb.op("tensor", lambda t: t.matmul(psRow[:, :], ones1[:, :],
                                               statrow[:, TT:2 * TT],
                                               start=True, stop=True),
                  reads=("statrow", "ones1"), writes=("psRow",))
            kb.op("scalar", lambda e: e.copy(bcA[:, :], psRow[:, :]),
                  reads=("psRow",), writes=("bcA",))

        def layernorm(src_p, sname, dst_p, dname):
            fns = [lambda t, c=c: t.matmul(psStat, onesK[:, :],
                                           plane(src_p, c),
                                           start=(c == 0), stop=(c == NC - 1))
                   for c in range(NC)]
            kb.op("tensor", fns, reads=(sname, "onesK"), writes=("psRow",))
            kb.op("vector", lambda e: e.tensor_scalar_mul(
                statrow[:, 0:TT], psStat, 1.0 / D),
                reads=("psRow",), writes=("statrow",))
            sq_chain(src_p, sname)
            ln_rest(src_p, sname, dst_p, dname, psStat, "psRow")

        def ln_finish(src_p, sname, dst_p, dname):
            kb.op("vector", lambda e: e.tensor_scalar_mul(
                statrow[:, 0:TT], psStat, 1.0 / D),
                reads=("psRow",), writes=("statrow",))
            ln_rest(src_p, sname, dst_p, dname, psT[0:1, 0:TT], "psT")

        def ln_rest(src_p, sname, dst_p, dname, s2row, s2name):
            kb.op("vector", lambda e: e.tensor_mul(
                statrow[:, TT:2 * TT], statrow[:, 0:TT], statrow[:, 0:TT]),
                reads=("statrow",), writes=("statrow",))
            kb.op("vector", lambda e, s2row=s2row: e.scalar_tensor_tensor(
                statrow[:, TT:2 * TT], s2row, 1.0 / D,
                statrow[:, TT:2 * TT], ALU.mult, ALU.subtract),
                reads=(s2name, "statrow"), writes=("statrow",))
            kb.op("vector", lambda e: e.tensor_scalar(
                statrow[:, TT:2 * TT], statrow[:, TT:2 * TT],
                1.0, 1e-5, ALU.mult, ALU.add),
                reads=("statrow",), writes=("statrow",))
            kb.op("scalar", lambda e: e.activation(
                statrow[:, 2 * TT:3 * TT], statrow[:, TT:2 * TT], AF.Ln),
                reads=("statrow",), writes=("statrow",))
            kb.op("scalar", lambda e: e.activation(
                statrow[:, 3 * TT:4 * TT], statrow[:, 2 * TT:3 * TT], AF.Exp,
                scale=-0.5),
                reads=("statrow",), writes=("statrow",))
            for _nr in range(1):
                kb.op("vector", lambda e: e.tensor_mul(
                    statrow[:, 2 * TT:3 * TT], statrow[:, TT:2 * TT],
                    statrow[:, 3 * TT:4 * TT]),
                    reads=("statrow",), writes=("statrow",))
                kb.op("vector", lambda e: e.tensor_mul(
                    statrow[:, 2 * TT:3 * TT], statrow[:, 2 * TT:3 * TT],
                    statrow[:, 3 * TT:4 * TT]),
                    reads=("statrow",), writes=("statrow",))
                kb.op("vector", lambda e: e.tensor_scalar(
                    statrow[:, 2 * TT:3 * TT], statrow[:, 2 * TT:3 * TT],
                    -0.5, 1.5, ALU.mult, ALU.add),
                    reads=("statrow",), writes=("statrow",))
                kb.op("vector", lambda e: e.tensor_mul(
                    statrow[:, 3 * TT:4 * TT], statrow[:, 3 * TT:4 * TT],
                    statrow[:, 2 * TT:3 * TT]),
                    reads=("statrow",), writes=("statrow",))
            kb.op("vector", lambda e: e.tensor_mul(
                statrow[:, 2 * TT:3 * TT], statrow[:, 0:TT],
                statrow[:, 3 * TT:4 * TT]),
                reads=("statrow",), writes=("statrow",))
            kb.op("tensor", lambda t: t.matmul(psRow[:, :], ones1[:, :],
                                               statrow[:, 3 * TT:4 * TT],
                                               start=True, stop=True),
                  reads=("statrow", "ones1"), writes=("psRow",))
            kb.op("scalar", lambda e: e.copy(bcA[:, :], psRow[:, :]),
                  reads=("psRow",), writes=("bcA",))
            kb.op("tensor", lambda t: t.matmul(psRow[:, :], ones1[:, :],
                                               statrow[:, 2 * TT:3 * TT],
                                               start=True, stop=True),
                  reads=("statrow", "ones1"), writes=("psRow",))
            kb.op("scalar", lambda e: e.copy(bcB[:, :], psRow[:, :]),
                  reads=("psRow",), writes=("bcB",))
            for c in range(NC):
                kb.op("vector", lambda e, c=c: e.tensor_mul(
                    tmpb[:, :], plane(src_p, c), bcA[:, :]),
                    reads=(sname, "bcA"), writes=("tmpb",))
                kb.op("vector", lambda e, c=c: e.tensor_sub(
                    plane(dst_p, c), tmpb[:, :], bcB[:, :]),
                    reads=("tmpb", "bcB"), writes=(dname,))

        for t in range(NT):
            sl = slice(t * TT, (t + 1) * TT)
            a1x, a1kv, a1wkv, a1k = P[0], P[1], P[2], P[3]
            load(a1x[:, :].rearrange("p (c j) -> p c j", c=NC),
                 xd[:, :, sl], ("P0",), sem="lP0")
            rms_stats(a1x, "P0")
            for c in range(NC):
                kb.op("vector", lambda e, c=c, a1x=a1x: e.tensor_mul(
                    tmpb[:, :], plane(a1x, c), bcA[:, :]),
                    reads=("P0", "bcA"), writes=("tmpb",))
                split16(tmpb[:, :], plane(sphi, c), plane(splo, c),
                        ("tmpb",), "sp")
            store(dnormh[:, :, sl],
                  sphi[:, :].rearrange("p (c j) -> p c j", c=NC),
                  ("sp",), (f"dnorm{t}",), sem="stN")
            store(dnorml[:, :, sl],
                  splo[:, :].rearrange("p (c j) -> p c j", c=NC),
                  ("sp",), (f"dnorm{t}",), sem="stN")
            for mc in range(NC):
                strip3(0, f"pwk{mc}", wkd[mc], pwk[mc])
                mm3((2 * mc) % 4, wsp[0][0], wsp[0][1],
                    lambda kc: plane(sphi, kc), lambda kc: plane(splo, kc),
                    NC, ("wsp0", "sp"))
                kb.op("scalar", lambda e, b=(2 * mc) % 4: e.copy(
                    ktmp[:, :], ps[b][:, :]),
                    reads=(f"ps{(2 * mc) % 4}",), writes=("ktmp",))
                strip3(1, f"pwv{mc}", wvd[mc], pwv[mc])
                mm3((2 * mc + 1) % 4, wsp[1][0], wsp[1][1],
                    lambda kc: plane(sphi, kc), lambda kc: plane(splo, kc),
                    NC, ("wsp1", "sp"))
                kb.op("vector", lambda e, mc=mc, b=(2 * mc + 1) % 4,
                      a1kv=a1kv: e.tensor_mul(
                    plane(a1kv, mc), ps[b][:, :], ktmp[:, :]),
                    reads=(f"ps{(2 * mc + 1) % 4}", "ktmp"),
                    writes=(f"P1_{mc}",))
                c = mc
                stn = f"sst{c}"
                kvn = f"P1_{c}"
                kb.op("vector", lambda e, c=c: e.tensor_copy(
                    scanbuf[:, 0:1], sstate[:, c:c + 1]),
                    reads=(stn,), writes=("scanbuf",))
                kb.op("vector", lambda e, c=c, a1wkv=a1wkv, a1kv=a1kv:
                      e.scalar_tensor_tensor(
                    plane(a1wkv, c)[:, 0:1], plane(a1kv, c)[:, 0:1],
                    eu[:, c:c + 1], scanbuf[:, 0:1], ALU.mult, ALU.add),
                    reads=(kvn, "eu", "scanbuf"), writes=(f"P2_{c}",))
                kb.op("vector", lambda e, c=c, a1kv=a1kv:
                      e.scalar_tensor_tensor(
                    plane(a1kv, c)[:, 0:1], sstate[:, c:c + 1],
                    ew[:, c:c + 1], plane(a1kv, c)[:, 0:1],
                    ALU.mult, ALU.add),
                    reads=(stn, "ew", kvn), writes=(kvn,))
                kb.op("vector", [
                    lambda e: e.drain(),
                    lambda e, c=c, a1kv=a1kv: e.tensor_tensor_scan(
                        out=scanbuf[:, 1:TT + 1],
                        data0=ew[:, c:c + 1].broadcast_to((128, TT)),
                        data1=plane(a1kv, c),
                        initial=0.0,
                        op0=ALU.mult, op1=ALU.add),
                    lambda e: e.drain(),
                    lambda e, c=c: e.tensor_copy(
                        sstate[:, c:c + 1], scanbuf[:, TT:TT + 1])],
                    reads=("ew", kvn), writes=("scanbuf", stn))
                kb.op("vector", lambda e, c=c, a1wkv=a1wkv, a1kv=a1kv:
                      e.scalar_tensor_tensor(
                    plane(a1wkv, c)[:, 1:TT], plane(a1kv, c)[:, 1:TT],
                    eu[:, c:c + 1], scanbuf[:, 1:TT], ALU.mult, ALU.add),
                    reads=(kvn, "eu", "scanbuf"), writes=(f"P2_{c}",))
            store(dwkv[:, :, sl], a1wkv[:, :].rearrange("p (c j) -> p c j", c=NC),
                  tuple(f"P2_{c}" for c in range(NC)), (f"dwkv{t}",),
                  sem="stW")

        store(cc1i[0, :].rearrange("(c p) -> p c", p=128), sstate[:, :],
              tuple(f"sst{c}" for c in range(NC)), ("cc1i",), slow=True,
              sem="stCC1")
        kb.op("gpsimd", lambda e: e.collective_compute(
            "AllGather", ALU.bypass, replica_groups=[list(range(8))],
            ins=[cc1i.opt()], outs=[cc1o.opt()]),
            reads=("cc1i",), writes=("cc1o",), sem="sCC", inc=1)
        for r_ in range(NC):
            load(s0raw[:, :].rearrange("p (c r) -> p c r", c=NC)[:, :, r_],
                 cc1o[r_, :].rearrange("(c p) -> p c", p=128),
                 ("s0raw",), reads=("cc1o",), slow=True, sem="lS0")
        kb.op("vector", lambda e: e.tensor_tensor(
            v3(s0raw[:, :], a=NC), v3(s0raw[:, :], a=NC),
            ssel[:, :].unsqueeze(1).broadcast_to((128, NC, E)),
            ALU.mult), reads=("s0raw", "ssel"), writes=("s0raw",))
        kb.op("vector", lambda e: e.reduce_sum(
            s0[:, :], v3(s0raw[:, :], a=NC), axis=AX.X),
            reads=("s0raw",), writes=("s0",))

        routers = []
        for t in range(NT):
            sl = slice(t * TT, (t + 1) * TT)
            a2x, a2wkv, a2x1, a2r = P[0], P[1], P[2], P[3]
            load(a2x[:, :].rearrange("p (c j) -> p c j", c=NC),
                 xd[:, :, sl], ("P0",), sem="lP0")
            load(sphi[:, :].rearrange("p (c j) -> p c j", c=NC),
                 dnormh[:, :, sl], ("sp",), reads=(f"dnorm{t}",), sem="lP1")
            load(splo[:, :].rearrange("p (c j) -> p c j", c=NC),
                 dnorml[:, :, sl], ("sp",), reads=(f"dnorm{t}",), sem="lP1")
            load(a2wkv[:, :].rearrange("p (c j) -> p c j", c=NC),
                 dwkv[:, :, sl], ("P1",), reads=(f"dwkv{t}",), sem="lP2")
            kb.op("gpsimd", lambda e, t=t: e.iota(
                rampb[:, :], pattern=[[1, TT]], base=t * TT,
                channel_multiplier=0, allow_small_or_imprecise_dtypes=True),
                writes=("rampb",), sem="sGE", inc=1)
            def r_chain(mc, rdst, rdstname):
                strip3(mc % 2, f"pwr{mc}", wrd[mc], pwr[mc])
                mm3(mc % 4, wsp[mc % 2][0], wsp[mc % 2][1],
                    lambda kc: plane(sphi, kc), lambda kc: plane(splo, kc),
                    NC, (f"wsp{mc % 2}", "sp"))
                kb.op("scalar", lambda e, b=mc % 4, rdst=rdst:
                      e.activation(rdst, ps[b][:, :], AF.Sigmoid),
                      reads=(f"ps{mc % 4}",), writes=(rdstname,))

            def corr_chunk(c, rsrc, rsrcname):
                kb.op("vector", lambda e, c=c: e.tensor_scalar(
                    tmpb[:, :], rampb[:, :], ne1[:, c:c + 1], None, ALU.mult),
                    reads=("rampb", "ne1"), writes=("tmpb",))
                kb.op("scalar", lambda e: e.activation(
                    tmpb[:, :], tmpb[:, :], AF.Exp),
                    reads=("tmpb",), writes=("tmpb",))
                kb.op("vector", lambda e, c=c, a2wkv=a2wkv:
                      e.scalar_tensor_tensor(
                    tmp2b[:, :], tmpb[:, :], s0[:, c:c + 1],
                    plane(a2wkv, c), ALU.mult, ALU.add),
                    reads=("tmpb", "s0", "P1"), writes=("tmp2b",))
                kb.op("vector", lambda e, rsrc=rsrc: e.tensor_mul(
                    tmpb[:, :], rsrc, tmp2b[:, :]),
                    reads=(rsrcname, "tmp2b"), writes=("tmpb",))
                split16(tmpb[:, :], SP16[:, c * TT:(c + 1) * TT],
                        SP16[:, NC * TT + c * TT:NC * TT + (c + 1) * TT],
                        ("tmpb",), "SP16")

            if t == 0:
                for mc in range(NC):
                    r_chain(mc, plane(a2r, mc), "P3")
                for c in range(NC):
                    corr_chunk(c, plane(a2r, c), "P3")
            else:
                for mc in range(NC):
                    r_chain(mc, rtmp[:, :], "rtmp")
                    corr_chunk(mc, rtmp[:, :], "rtmp")
            if routers:
                routers.pop(0)()
            for mc in range(NC):
                strip3(mc % 2, f"pwo{mc}", wod[mc], pwo[mc])
                mm3(mc % 4, wsp[mc % 2][0], wsp[mc % 2][1],
                    lambda kc: SP16[:, kc * TT:(kc + 1) * TT],
                    lambda kc: SP16[:, NC * TT + kc * TT:
                                    NC * TT + (kc + 1) * TT],
                    NC, (f"wsp{mc % 2}", "SP16"))
                kb.op("vector", lambda e, mc=mc, b=mc % 4, a2x1=a2x1,
                      a2x=a2x: e.scalar_tensor_tensor(
                    plane(a2x1, mc), ps[b][:, :], mxs[:, mc:mc + 1],
                    plane(a2x, mc), ALU.mult, ALU.add),
                    reads=(f"ps{mc % 4}", "vecs", "P0"), writes=("P2",))
                bufs = (tmpb, tmp2b)
                names = ("tmpb", "tmp2b")
                kb.op("scalar", lambda e, b=bufs[mc % 2], a2x1=a2x1, mc=mc:
                      e.activation(b[:, :], plane(a2x1, mc), AF.Square),
                      reads=("P2",), writes=(names[mc % 2],))
                kb.op("tensor", lambda t_, mc=mc, b=bufs[mc % 2]: t_.matmul(
                    psStat, onesK[:, :], b[:, :],
                    start=(mc == 0), stop=(mc == NC - 1)),
                    reads=(names[mc % 2], "onesK"), writes=("psRow",))
            if debug and t == 0:
                store(dbg["dbg_x1"], a2x1[:, :].rearrange(
                    "p (c j) -> p c j", c=NC), ("P2",))
            rms_finish()
            for c in range(NC):
                kb.op("vector", lambda e, c=c, a2x1=a2x1: e.tensor_mul(
                    tmpb[:, :], plane(a2x1, c), bcA[:, :]),
                    reads=("P2", "bcA"), writes=("tmpb",))
                split16(tmpb[:, :], plane(sphi, c), plane(splo, c),
                        ("tmpb",), "sp")
            for half in range(2):
                for h16 in range(16):
                    hk = half * 16 + h16
                    strip3(hk % 2, f"pwfc{hk}", wfcd[hk], pwfc[hk])
                    mm3(hk % 4, wsp[hk % 2][0], wsp[hk % 2][1],
                        lambda kc: plane(sphi, kc), lambda kc: plane(splo, kc),
                        NC, (f"wsp{hk % 2}", "sp"))
                    kb.op("scalar", lambda e, b=hk % 4: e.copy(
                        tmpb[:, :], ps[b][:, :]),
                        reads=(f"ps{hk % 4}",), writes=("tmpb",))
                    kb.op("vector", lambda e: e.scalar_tensor_tensor(
                        tmpb[:, :], tmpb[:, :], 0.5, tmpb[:, :], ALU.mult,
                        ALU.max), reads=("tmpb",), writes=("tmpb",))
                    kb.op("vector", lambda e: e.tensor_mul(
                        tmp2b[:, :], tmpb[:, :], tmpb[:, :]),
                        reads=("tmpb",), writes=("tmp2b",))
                    kb.op("scalar", lambda e, h16=h16: e.copy(
                        Ghi[:, h16 * TT:(h16 + 1) * TT], tmp2b[:, :]),
                        reads=("tmp2b",), writes=("SP16",))
                    kb.op("vector", lambda e, h16=h16: e.scalar_tensor_tensor(
                        Glo[:, h16 * TT:(h16 + 1) * TT],
                        Ghi[:, h16 * TT:(h16 + 1) * TT], -1.0, tmp2b[:, :],
                        ALU.mult, ALU.add),
                        reads=("tmp2b", "SP16"), writes=("SP16",))
                for dc in range(NC):
                    for q in range(2):
                        qi = q % 2
                        strip3(qi, f"pwmlp{dc}_{half}_{q}",
                               wmlpd[dc][:, half * 16 + q * 8:
                                         half * 16 + (q + 1) * 8, :],
                               pwmlp[dc, half, q])
                        mm3(qi, wsp[qi][0], wsp[qi][1],
                            lambda kc, q=q: Ghi[:, (q * 8 + kc) * TT:
                                                (q * 8 + kc + 1) * TT],
                            lambda kc, q=q: Glo[:, (q * 8 + kc) * TT:
                                                (q * 8 + kc + 1) * TT],
                            NC, (f"wsp{qi}", "SP16"))
                        if half == 0 and q == 0:
                            kb.op("vector", lambda e, dc=dc, b=qi,
                                  a2wkv=a2wkv, a2x1=a2x1:
                                  e.scalar_tensor_tensor(
                                plane(a2wkv, dc), ps[b][:, :],
                                mls[:, dc:dc + 1], plane(a2x1, dc),
                                ALU.mult, ALU.add),
                                reads=(f"ps{qi}", "vecs", "P2"),
                                writes=("P1",))
                        else:
                            kb.op("vector", lambda e, dc=dc, b=qi,
                                  a2wkv=a2wkv: e.scalar_tensor_tensor(
                                plane(a2wkv, dc), ps[b][:, :],
                                mls[:, dc:dc + 1], plane(a2wkv, dc),
                                ALU.mult, ALU.add),
                                reads=(f"ps{qi}", "vecs", "P1"),
                                writes=("P1",))
                        if half == 1 and q == 1:
                            kb.op("vector", lambda e, dc=dc, a2x=a2x,
                                  a2wkv=a2wkv: e.tensor_add(
                                plane(a2x, dc), plane(a2wkv, dc),
                                plane(a2x, dc)),
                                reads=("P1", "P0"), writes=("P0",))
                            kb.op("tensor", lambda t_, dc=dc, a2x=a2x:
                                  t_.matmul(
                                psStat, onesK[:, :], plane(a2x, dc),
                                start=(dc == 0), stop=(dc == NC - 1)),
                                reads=("P0", "onesK"), writes=("psRow",))
                            bufs = (tmpb, tmp2b)
                            names = ("tmpb", "tmp2b")
                            kb.op("scalar", lambda e, b=bufs[dc % 2],
                                  a2x=a2x, dc=dc: e.activation(
                                b[:, :], plane(a2x, dc), AF.Square),
                                reads=("P0",), writes=(names[dc % 2],))
                            kb.op("tensor", lambda t_, dc=dc,
                                  b=bufs[dc % 2]: t_.matmul(
                                psT[0:1, 0:TT], onesK[:, :], b[:, :],
                                start=(dc == 0), stop=(dc == NC - 1)),
                                reads=(names[dc % 2], "onesK"),
                                writes=("psT",))
            ln_finish(a2x, "P0", a2r, "P3")
            store(dxa[:, :, sl], a2r[:, :].rearrange("p (c j) -> p c j", c=NC),
                  ("P3",), (f"dxa{t}",), sem="stXa")
            if debug and t == 0:
                store(dbg["dbg_xa"], a2r[:, :].rearrange(
                    "p (c j) -> p c j", c=NC), ("P3",))

            def emit_router(t=t, a2r=a2r):
                fns = [lambda t_, kc=kc, a2r=a2r: t_.matmul(
                    ps[5][0:E, :], rwsb[:, kc * E:(kc + 1) * E],
                    plane(a2r, kc),
                    start=(kc == 0), stop=(kc == NC - 1))
                    for kc in range(NC)]
                kb.op("tensor", fns, reads=("rw", "P3"), writes=("ps5",))
                kb.op("vector", lambda e: e.tensor_scalar(
                    logits[:, :], ps[5][0:E, :], rbsb[:, 0:1], None, ALU.add),
                    reads=("ps5", "rb"), writes=("logits",))
                fns = []
                for jj in range(4):
                    fns.append(lambda t_, jj=jj: t_.matmul(
                        psT[:, jj * 8:(jj + 1) * 8],
                        logits[:, jj * 128:(jj + 1) * 128], ident[0:E, 0:E],
                        is_transpose=True, start=(jj == 0), stop=(jj == 3)))
                kb.op("tensor", fns, reads=("logits", "ident"),
                      writes=("psT",))
                kb.op("vector", lambda e, t=t: e.tensor_copy(
                    ltm[:, t * 32:(t + 1) * 32], psT[:, 0:32]),
                    reads=("psT",), writes=("ltm",))

            routers.append(emit_router)

        while routers:
            routers.pop(0)()

        if debug:
            store(dbg["dbg_ltm"], ltm[:, :], ("ltm",))
        kb.op("vector", lambda e: e.reduce_max(rowm[:, :], v3(ltm[:, :]),
                                               axis=AX.X),
              reads=("ltm",), writes=("rowm",))
        kb.op("vector", lambda e: e.tensor_tensor(
            v3(uu[:, :]), v3(ltm[:, :]),
            rowm[:, :].unsqueeze(2).broadcast_to((128, 16, 8)),
            ALU.subtract), reads=("ltm", "rowm"), writes=("uu",))
        kb.op("scalar", lambda e: e.activation(uu[:, :], uu[:, :], AF.Exp),
              reads=("uu",), writes=("uu",))
        kb.op("vector", lambda e: e.reduce_max(rowm[:, :], v3(uu[:, :]),
                                               axis=AX.X),
              reads=("uu",), writes=("rowm",))
        kb.op("vector", lambda e: e.tensor_tensor(
            v3(eq1[:, :]), v3(uu[:, :]),
            rowm[:, :].unsqueeze(2).broadcast_to((128, 16, 8)),
            ALU.is_equal), reads=("uu", "rowm"), writes=("eq1",))
        kb.op("vector", lambda e: e.tensor_mul(eq2[:, :], uu[:, :], eq1[:, :]),
              reads=("uu", "eq1"), writes=("eq2",))
        kb.op("vector", lambda e: e.tensor_sub(eq2[:, :], uu[:, :], eq2[:, :]),
              reads=("uu", "eq2"), writes=("eq2",))
        kb.op("vector", lambda e: e.reduce_max(rowm2[:, :], v3(eq2[:, :]),
                                               axis=AX.X),
              reads=("eq2",), writes=("rowm2",))
        kb.op("vector", lambda e: e.tensor_tensor(
            v3(eq2[:, :]), v3(eq2[:, :]),
            rowm2[:, :].unsqueeze(2).broadcast_to((128, 16, 8)),
            ALU.is_equal), reads=("eq2", "rowm2"), writes=("eq2",))
        kb.op("vector", lambda e: e.tensor_add(roww[:, 0:16], rowm[:, :],
                                               rowm2[:, :]),
              reads=("rowm", "rowm2"), writes=("roww",))
        kb.op("vector", lambda e: e.reciprocal(roww[:, 0:16], roww[:, 0:16]),
              reads=("roww",), writes=("roww",))
        kb.op("vector", lambda e: e.tensor_mul(roww[:, 16:32], rowm[:, :],
                                               roww[:, 0:16]),
              reads=("rowm", "roww"), writes=("roww",))
        kb.op("vector", lambda e: e.tensor_mul(roww[:, 32:48], rowm2[:, :],
                                               roww[:, 0:16]),
              reads=("rowm2", "roww"), writes=("roww",))
        a01i = a01[:, :].rearrange("p (a kb) -> p a kb", a=16)
        kb.op("vector", lambda e: e.tensor_tensor(
            a01i[:, :, 0:8], v3(eq1[:, :]),
            roww[:, 16:32].unsqueeze(2).broadcast_to((128, 16, 8)),
            ALU.mult), reads=("eq1", "roww"), writes=("a01",))
        kb.op("vector", lambda e: e.tensor_tensor(
            a01i[:, :, 8:16], v3(eq2[:, :]),
            roww[:, 32:48].unsqueeze(2).broadcast_to((128, 16, 8)),
            ALU.mult), reads=("eq2", "roww"), writes=("a01",))
        kb.op("vector", lambda e: e.tensor_tensor(
            v3(masktm[:, :]), a01i[:, :, 0:8], a01i[:, :, 8:16],
            ALU.add), reads=("a01",), writes=("masktm",))
        fns = [lambda t_, jj=jj: t_.matmul(
            psT[0:16, 0:1], a01[:, jj * 16:(jj + 1) * 16], onesK[:, :],
            start=(jj == 0), stop=(jj == 15)) for jj in range(16)]
        kb.op("tensor", fns, reads=("a01", "onesK"), writes=("psT",))
        kb.op("vector", lambda e: e.tensor_copy(tcol[:, 0:1], psT[0:16, 0:1]),
              reads=("psT",), writes=("tcol",))
        kb.op("tensor", lambda t_: t_.matmul(
            psT[0:1, 0:16], tcol[:, 0:1], ident[0:16, 0:16],
            is_transpose=True, start=True, stop=True),
            reads=("tcol", "ident"), writes=("psT",))
        kb.op("scalar", lambda e: e.copy(tpad[:, 0:16], psT[0:1, 0:16]),
              reads=("psT",), writes=("tpad",))
        store(cc2i[0:1, :], tpad[0:1, :], ("tpad",), ("cc2i",), sem="stCC2")
        kb.op("gpsimd", lambda e: e.collective_compute(
            "AllGather", ALU.bypass, replica_groups=[list(range(8))],
            ins=[cc2i.opt()], outs=[cc2o.opt()]),
            reads=("cc2i",), writes=("cc2o",), sem="sCC", inc=1)

        for grp in range(4):
            fns = []
            for j4 in range(4):
                jj = grp * 4 + j4
                fns.append(lambda t_, jj=jj, j4=j4: t_.matmul(
                    psT[0:E, j4 * 128:(j4 + 1) * 128],
                    masktm[:, jj * 8:(jj + 1) * 8], ident[:, :],
                    is_transpose=True, start=(j4 == 0), stop=(j4 == 3)))
            kb.op("tensor", fns, reads=("masktm", "ident"), writes=("psT",))
            kb.op("vector", lambda e, grp=grp: e.tensor_copy(
                maskem[:, grp * TT:(grp + 1) * TT], psT[0:E, 0:TT]),
                reads=("psT",), writes=("maskem",))

        kb.alias("SP16A", "SP16")
        kb.alias("SP16B", "SP16")

        def pass_b_tile(t):
            sl = slice(t * TT, (t + 1) * TT)
            bxa, bmacc, by = P[0], P[1], P[2]
            load(bxa[:, :].rearrange("p (c j) -> p c j", c=NC),
                 dxa[:, :, sl], ("P0",), reads=(f"dxa{t}",), sem="lP0")
            for c in range(NC):
                split16(plane(bxa, c), plane(sphi, c), plane(splo, c),
                        ("P0",), "sp")
            def w1_block(e_, bank):
                h2hi, h2lo = h2banks[bank]
                bname = "SP16A" if bank == 0 else "SP16B"
                kb.op("tensor", lambda t_, e_=e_, sl=sl: t_.matmul(
                    ps[5][:, :], esel[:, e_ * 128:(e_ + 1) * 128],
                    maskem[0:E, sl], start=True, stop=True),
                    reads=("maskem", "esel"), writes=("ps5",))
                kb.op("scalar", lambda e: e.copy(maskb[:, :], ps[5][:, :]),
                      reads=("ps5",), writes=("maskb",))
                for hk in range(NC):
                    strip3(hk % 2, f"pw1_{e_}_{hk}", w1d[e_, hk],
                           pw1[e_, hk])
                    mm3(hk % 4, wsp[hk % 2][0], wsp[hk % 2][1],
                        lambda kc: plane(sphi, kc), lambda kc: plane(splo, kc),
                        NC, (f"wsp{hk % 2}", "sp"))
                    kb.op("vector", lambda e, b=hk % 4: e.tensor_mul(
                        tmpb[:, :], ps[b][:, :], maskb[:, :]),
                        reads=(f"ps{hk % 4}", "maskb"), writes=("tmpb",))
                    kb.op("scalar", lambda e, e_=e_, hk=hk: e.activation(
                        tmpb[:, :], tmpb[:, :], AF.Gelu,
                        bias=b1v[:, e_ * 8 + hk:e_ * 8 + hk + 1]),
                        reads=("tmpb", "b1v"), writes=("tmpb",))
                    kb.op("vector", lambda e, hk=hk, h2hi=h2hi: e.tensor_mul(
                        h2hi[:, hk * TT:(hk + 1) * TT], tmpb[:, :],
                        maskb[:, :]),
                        reads=("tmpb", "maskb"), writes=(bname,))
                    kb.op("vector", lambda e: e.tensor_mul(
                        tmp2b[:, :], tmpb[:, :], maskb[:, :]),
                        reads=("tmpb", "maskb"), writes=("tmp2b",))
                    kb.op("vector", lambda e, hk=hk, h2hi=h2hi, h2lo=h2lo:
                          e.scalar_tensor_tensor(
                        h2lo[:, hk * TT:(hk + 1) * TT],
                        h2hi[:, hk * TT:(hk + 1) * TT], -1.0, tmp2b[:, :],
                        ALU.mult, ALU.add),
                        reads=("tmp2b", bname), writes=(bname,))

            def w2_block(e_, bank):
                h2hi, h2lo = h2banks[bank]
                bname = "SP16A" if bank == 0 else "SP16B"
                for dc in range(NC):
                    strip3(dc % 2, f"pw2_{e_}_{dc}", w2d[e_, dc],
                           pw2[e_, dc], bf=True)
                    mm3(dc % 4, wspb[dc % 2][0], wspb[dc % 2][1],
                        lambda kc, h2hi=h2hi: h2hi[:, kc * TT:(kc + 1) * TT],
                        lambda kc, h2lo=h2lo: h2lo[:, kc * TT:(kc + 1) * TT],
                        NC, (f"wsp{dc % 2}", bname))
                    if e_ == 0:
                        kb.op("vector", lambda e, dc=dc, b=dc % 4,
                              bmacc=bmacc: e.tensor_copy(
                              plane(bmacc, dc), ps[b][:, :]),
                              reads=(f"ps{dc % 4}",), writes=("P1",))
                    else:
                        kb.op("vector", lambda e, dc=dc, b=dc % 4,
                              bmacc=bmacc: e.tensor_add(
                              plane(bmacc, dc), ps[b][:, :],
                              plane(bmacc, dc)),
                              reads=(f"ps{dc % 4}", "P1"), writes=("P1",))

            for e_ in range(E):
                w1_block(e_, e_ % 2)
                if e_ > 0:
                    w2_block(e_ - 1, (e_ - 1) % 2)
            w2_block(E - 1, (E - 1) % 2)
            for dc in range(NC):
                kb.op("tensor", lambda t_, dc=dc, sl=sl: t_.matmul(
                    ps[dc % 4][:, :], b2r[:, dc * 128:(dc + 1) * 128],
                    maskem[0:E, sl], start=True, stop=True),
                    reads=("b2r", "maskem"), writes=(f"ps{dc % 4}",))
                kb.op("vector", lambda e, dc=dc, b=dc % 4, bmacc=bmacc:
                      e.tensor_add(plane(bmacc, dc), ps[b][:, :],
                                   plane(bmacc, dc)),
                      reads=(f"ps{dc % 4}", "P1"), writes=("P1",))
            if debug and t == 0:
                store(dbg["dbg_macc"], bmacc[:, :].rearrange(
                    "p (c j) -> p c j", c=NC), ("P1",))
            for c in range(NC):
                kb.op("vector", lambda e, c=c, bmacc=bmacc, bxa=bxa:
                      e.tensor_add(
                    plane(bmacc, c), plane(bmacc, c), plane(bxa, c)),
                    reads=("P1", "P0"), writes=("P1",))
            layernorm(bmacc, "P1", by, "P2")
            store(yd[:, :, sl], by[:, :].rearrange("p (c j) -> p c j", c=NC),
                  ("P2",), (f"ydone{t}",), sem="stY")

        for t in (1, 2, 3):
            pass_b_tile(t)

        load(trow[:, :], cc2o.transpose([1, 0])[0:16, :], ("trow",),
             reads=("cc2o",), slow=True, sem="lT")
        kb.op("vector", lambda e: e.reduce_sum(tcol[:, 0:1], trow[:, :],
                                               axis=AX.X),
              reads=("trow",), writes=("tcol",))
        kb.op("gpsimd", lambda e: e.dma_start(
            out=fixem[:, :], in_=tcol[:, 0:1],
            allow_slow_non_contiguous=True),
            reads=("tcol",), writes=("fixem",), sem="lFix")
        kb.op("vector", lambda e: e.tensor_scalar(
            fixem[:, :], fixem[:, :], fixg[:, 0:1], None, ALU.mult),
            reads=("fixem", "fixg"), writes=("fixem",))
        kb.op("vector", lambda e: e.tensor_tensor(
            maskem[0:2, 0:8], maskem[0:2, 0:8], fixem[:, :], ALU.add),
            reads=("maskem", "fixem"), writes=("maskem",))
        kb.op("vector", lambda e: e.tensor_scalar_min(
            maskem[0:2, 0:8], maskem[0:2, 0:8], 2560.0),
            reads=("maskem",), writes=("maskem",))
        if debug:
            store(dbg["dbg_maskem"], maskem[:, :], ("maskem",))

        pass_b_tile(0)

        vy = kb.cnt["stY"]
        kb.streams["gpsimd"].append(
            lambda e, v=vy: e.wait_ge(sems["stY"], v))
        if debug:
            vd = kb.cnt["stDbg"]
            kb.streams["gpsimd"].append(
                lambda e, v=vd: e.wait_ge(sems["stDbg"], v))

        with nc.Block() as block:
            @block.sync
            def _(eng):
                for fn in kb.streams["sync"]:
                    fn(eng)

            @block.gpsimd
            def _(eng):
                for fn in kb.streams["gpsimd"]:
                    fn(eng)

            @block.tensor
            def _(eng):
                for fn in kb.streams["tensor"]:
                    fn(eng)

            @block.scalar
            def _(eng):
                for fn in kb.streams["scalar"]:
                    fn(eng)

            @block.vector
            def _(eng):
                for fn in kb.streams["vector"]:
                    fn(eng)

    return nc


def _tile4(wt, nk, nm):
    return np.ascontiguousarray(
        wt.reshape(nk, 128, nm, 128).transpose(2, 1, 0, 3))


_CACHE = {}
_LAST_RES = None


def kernel(**inputs):
    f32 = np.float32
    x = np.asarray(inputs["L0_attn_x"], f32)
    wk = _tile4(np.ascontiguousarray(inputs["L0_attn_Wk"].T.astype(f32)), NC, NC)
    wv = _tile4(np.ascontiguousarray(inputs["L0_attn_Wv"].T.astype(f32)), NC, NC)
    wr = _tile4(np.ascontiguousarray(inputs["L0_attn_Wr"].T.astype(f32)), NC, NC)
    wo = _tile4(np.ascontiguousarray(inputs["L0_attn_Wo"].T.astype(f32)), NC, NC)
    wfc = _tile4(np.ascontiguousarray(inputs["L0_attn_Wfc"].T.astype(f32)), NC, NHC)
    wmlp = _tile4(np.ascontiguousarray(inputs["L0_attn_Wmlp"].T.astype(f32)), NHC, NC)
    w1 = np.ascontiguousarray(
        np.asarray(inputs["L0_ffn_w1"], f32).reshape(E, NC, 128, NC, 128)
        .transpose(0, 3, 2, 1, 4))
    w2 = np.ascontiguousarray(
        np.asarray(inputs["L0_ffn_w2"], f32).reshape(E, NC, 128, NC, 128)
        .transpose(0, 3, 2, 1, 4))
    rw = np.ascontiguousarray(
        np.asarray(inputs["L0_ffn_router_w"], f32).reshape(NC, 128, E)
        .transpose(1, 0, 2))
    b2r = np.ascontiguousarray(np.asarray(inputs["L0_ffn_b2"], f32))
    b1 = np.asarray(inputs["L0_ffn_b1"], f32)
    b1v = np.ascontiguousarray(
        np.concatenate([b1[e].reshape(8, 128).T for e in range(E)], axis=1))
    packv = lambda v: np.asarray(v, f32).reshape(8, 128).T
    vecs = np.ascontiguousarray(np.concatenate(
        [packv(inputs["L0_attn_time_decay"]),
         packv(inputs["L0_attn_time_first"]),
         packv(inputs["L0_attn_mixer_scale"]),
         packv(inputs["L0_attn_mlp_scale"])], axis=1))
    rb = np.ascontiguousarray(
        np.asarray(inputs["L0_ffn_router_b"], f32).reshape(E, 1))
    ident = np.eye(128, dtype=f32)
    esel = np.zeros((E, E * 128), f32)
    for e in range(E):
        esel[e, e * 128:(e + 1) * 128] = 1.0

    if _CACHE.get("dbg") != DEBUG:
        _CACHE["nc"] = build_program(debug=DEBUG)
        _CACHE["dbg"] = DEBUG
    nc = _CACHE["nc"]

    in_maps = []
    for core in range(8):
        b, h = core // 2, core % 2
        x_fm = np.ascontiguousarray(x[b, h * TPC:(h + 1) * TPC, :].T)
        xcore = np.ascontiguousarray(
            x_fm.reshape(NC, 128, TPC).transpose(1, 0, 2))
        sselrow = np.zeros(E, f32)
        if h == 1:
            sselrow[2 * b] = 1.0
        ssel = np.ascontiguousarray(np.tile(sselrow, (128, 1)))
        fixgv = np.full((2, 1), 1.0 if core == 0 else 0.0, f32)
        in_maps.append({
            "x": xcore, "wk": wk, "wv": wv, "wr": wr, "wo": wo,
            "wfc": wfc, "wmlp": wmlp, "w1": w1, "w2": w2, "rw": rw,
            "b2r": b2r, "b1v": b1v, "vecs": vecs, "rb": rb,
            "ident": ident, "esel": esel, "ssel": ssel, "fixg": fixgv,
        })

    res = run_bass_kernel_spmd(nc, in_maps, core_ids=list(range(8)))
    _CACHE["res"] = res
    global _LAST_RES
    _LAST_RES = res

    out = np.empty((4, 4096, D), f32)
    for core in range(8):
        b, h = core // 2, core % 2
        ycore = res.results[core]["y"]
        y_fm = ycore.transpose(1, 0, 2).reshape(D, TPC)
        out[b, h * TPC:(h + 1) * TPC, :] = y_fm.T
    return out
